# version 06 (final)
# baseline (speedup 1.0000x reference)
import numpy as np
import ml_dtypes
import concourse.bass as bass
import concourse.mybir as mybir
from concourse.bass_utils import run_bass_kernel_spmd

F32 = mybir.dt.float32
BF16 = mybir.dt.bfloat16
I32 = mybir.dt.int32
AF = mybir.ActivationFunctionType
ALU = mybir.AluOpType
AX = mybir.AxisListType

NCORE = 8
T = 1024
D = 2048
NCH = D // 128
I0 = 8192
I1 = 1408
NE = 8
GROUPS = [[0, 1], [2, 3], [4, 5], [6, 7]]
EPS = 1e-5
NEG = -1e30



class EngCtx:
    def __init__(self, name, sem, step=1, is_dma=False):
        self.name = name
        self.sem = sem
        self.step = step
        self.is_dma = is_dma
        self.n = 0


class Prog:

    ENGS = ("sync", "gpsimd", "vector", "scalar", "tensor")

    def __init__(self, nc, ctxs):
        self.nc = nc
        self.ctxs = ctxs
        self.thunks = {k: [] for k in self.ENGS}
        self.watermark = {k: {} for k in self.ENGS}

    def run(self, eng, fn, waits=(), inc=None, inc_k=None):
        th = self.thunks[eng]
        wm = self.watermark[eng]
        for w in waits:
            if w is None:
                continue
            src, val = w
            if wm.get(src.name, 0) >= val:
                continue
            wm[src.name] = val
            th.append(lambda e, s=src.sem, v=val: e.wait_ge(s, v))
        tok = None
        if inc is not None:
            k = inc_k if inc_k is not None else inc.step
            inc.n += k
            tok = (inc, inc.n)
            th.append(lambda e, f=fn, s=inc.sem, kk=k: f(e).then_inc(s, kk))
        else:
            th.append(lambda e, f=fn: f(e))
        return tok

    def barrier_wait(self, eng, toks):
        self.run(eng, lambda e: e.engine_nop() if eng in ("vector", "gpsimd") else e.nop(),
                 waits=toks)

    def emit(self):
        blk_mgr = self.nc.Block(no_gpsimd_drain=True)
        with blk_mgr as block:
            for name in self.ENGS:
                ths = self.thunks[name]
                if not ths:
                    continue
                section = getattr(block, name)

                def body(e, ths=ths):
                    for t in ths:
                        t(e)
                section(body)
        self.thunks = {k: [] for k in self.ENGS}
        self.watermark = {k: {} for k in self.ENGS}



def ceil_div(a, b):
    return (a + b - 1) // b


_UID = [0]


def uname(tag):
    _UID[0] += 1
    return f"{tag}{_UID[0]}"


class Bufs(dict):
    __getattr__ = dict.__getitem__
    __setattr__ = dict.__setitem__


def setup_consts(P, b):
    dve = P.ctxs["dve"]
    P.run("vector", lambda e: e.memset(b.ones_col_bf[:], 1.0))
    P.run("vector", lambda e: e.memset(b.ones_col_f32[:], 1.0))
    P.run("vector", lambda e: e.memset(b.ones_row_bf[:], 1.0))
    P.run("vector", lambda e: e.memset(b.ones_row_f32[:], 1.0))
    P.run("vector", lambda e: e.memset(b.neg128_row[:], -128.0))
    P.run("vector", lambda e: e.memset(b.c128_row[:], 128.0))
    P.run("vector", lambda e: e.memset(b.one_one[:], 1.0))
    t = P.run("vector", lambda e: e.memset(b.eps[:], EPS), inc=dve)
    return t



def emit_ln(P, b, st, x, xn, tok_x):
    pe, act, dve = P.ctxs["pe"], P.ctxs["act"], P.ctxs["dve"]
    cp_toks = []
    for c in range(NCH):
        sl = slice(c * T, (c + 1) * T)
        t1 = P.run("vector", lambda e, sl=sl, c=c: e.tensor_copy(
            st.cbuf[:, (c % 2) * T:(c % 2) * T + T], x[:, sl]),
            waits=[tok_x, st.cbuf_free[c % 2]], inc=dve)
        t2 = P.run("scalar", lambda e, sl=sl, c=c: e.activation(
            st.qbuf[:, (c % 2) * T:(c % 2) * T + T], x[:, sl], AF.Square),
            waits=[tok_x, st.qbuf_free[c % 2]], inc=act)
        for hh in range(2):
            hsl = slice((c % 2) * T + hh * 512, (c % 2) * T + hh * 512 + 512)
            osl = slice(hh * 512, hh * 512 + 512)
            P.run("tensor", lambda e, hsl=hsl, osl=osl, c=c: e.matmul(
                st.ps_s1[:, osl], b.ones_col_bf[:], st.cbuf[:, hsl],
                start=(c == 0), stop=(c == NCH - 1)),
                waits=[t1])
            tok_mm = P.run("tensor", lambda e, hsl=hsl, osl=osl, c=c: e.matmul(
                st.ps_s2[:, osl], b.ones_col_bf[:], st.qbuf[:, hsl],
                start=(c == 0), stop=(c == NCH - 1)),
                waits=[t2], inc=pe)
        st.cbuf_free[c % 2] = tok_mm
        st.qbuf_free[c % 2] = tok_mm
    tm = P.run("vector", lambda e: e.tensor_scalar_mul(st.row_m[:], st.ps_s1[:, :], 1.0 / D),
               waits=[tok_mm], inc=dve)
    te = P.run("vector", lambda e: e.tensor_scalar_mul(st.row_a[:], st.ps_s2[:, :], 1.0 / D),
               waits=[tok_mm], inc=dve)
    tq = P.run("scalar", lambda e: e.activation(st.row_b[:], st.row_m[:], AF.Square),
               waits=[tm], inc=act)
    tv = P.run("vector", lambda e: e.tensor_sub(st.row_a[:], st.row_a[:], st.row_b[:]),
               waits=[te, tq], inc=dve)
    ts = P.run("scalar", lambda e: e.activation(st.row_b[:], st.row_a[:], AF.Sqrt, bias=b.eps[0:1, 0:1]),
               waits=[tv], inc=act)
    tr = P.run("vector", lambda e: e.reciprocal(st.row_r[:], st.row_b[:]),
               waits=[ts], inc=dve)
    st.tok_m = tm
    st.tok_r = tr
    tmr = P.run("vector", lambda e: e.tensor_mul(st.row_mr[:], st.row_m[:], st.row_r[:]),
                waits=[tr], inc=dve)
    br_toks = []
    for hh in range(2):
        osl = slice(hh * 512, hh * 512 + 512)
        P.run("tensor", lambda e, osl=osl: e.matmul(
            st.ps_br[:][:, osl], b.ones_row_f32[:], st.row_r[0:1, osl],
            start=True, stop=True), waits=[tr])
        tb = P.run("tensor", lambda e, osl=osl: e.matmul(
            st.ps_bmr[:][:, osl], b.ones_row_f32[:], st.row_mr[0:1, osl],
            start=True, stop=True), waits=[tmr], inc=pe)
    xn_toks = []
    for c in range(NCH):
        sl = slice(c * T, (c + 1) * T)
        P.run("vector", lambda e, sl=sl, c=c: e.tensor_mul(
            st.cbuf[:, (c % 2) * T:(c % 2) * T + T], x[:, sl], st.ps_br[:][:, :]),
            waits=[tb, st.cbuf_free[c % 2]])
        tc2 = P.run("vector", lambda e, sl=sl, c=c: e.tensor_sub(
            xn[:, sl], st.cbuf[:, (c % 2) * T:(c % 2) * T + T], st.ps_bmr[:][:, :]),
            inc=dve)
        st.cbuf_free[c % 2] = tc2
        xn_toks.append(tc2)
    return xn_toks, tc2


def alloc_ln_sbuf(nc, es):
    st = Bufs()
    st.cbuf = es(nc.sbuf_tensor("ln_cbuf", [128, 2 * T], BF16))
    st.qbuf = es(nc.sbuf_tensor("ln_qbuf", [128, 2 * T], BF16))
    st.row_m = es(nc.sbuf_tensor("row_m", [1, T], F32))
    st.row_a = es(nc.sbuf_tensor("row_a", [1, T], F32))
    st.row_b = es(nc.sbuf_tensor("row_b", [1, T], F32))
    st.row_r = es(nc.sbuf_tensor("row_r", [1, T], F32))
    st.row_mr = st.row_a
    return st


def ln_psum_views(st, pa, pb):
    st.cbuf_free = [None, None]
    st.qbuf_free = [None, None]
    st.ps_s1 = pa[0:1, :]
    st.ps_s2 = pb[0:1, :]
    st.ps_br = pa
    st.ps_bmr = pb
    return st



def emit_ffn0(P, b, es, pspool, x, xn, xn_toks, w, half_scale=0.5):
    nc = P.nc
    pe, act, dve, hw, sw = (P.ctxs[k] for k in ("pe", "act", "dve", "hw", "sw"))
    xn_done = xn_toks[-1]

    NB = 16
    NBATCH = 2
    NGRP = I0 // (NB * 128)
    GW = NBATCH * 128
    DNB = 8

    wg = es(nc.sbuf_tensor(uname("s"), [128, 2 * NCH * GW], BF16))
    wd = es(nc.sbuf_tensor(uname("s"), [128, 2 * DNB * 512], BF16))
    sgb = es(nc.sbuf_tensor(uname("s"), [128, NBATCH * T], BF16))
    actb = es(nc.sbuf_tensor(uname("s"), [128, NB * T], BF16))
    ps = pspool
    ps_row = ps[0][0:1, :]
    rows = es(nc.sbuf_tensor(uname("s"), [1, 2 * T], F32))
    scl = es(nc.sbuf_tensor(uname("s"), [128, 64 + 64 + 16 + 16], F32))

    P.run("sync", lambda e: e.dma_start(out=scl[:, 0:64], in_=w["sg"][:, :]), inc=hw)
    P.run("sync", lambda e: e.dma_start(out=scl[:, 64:128], in_=w["su"][:, :]), inc=hw)
    P.run("sync", lambda e: e.dma_start(out=scl[:, 128:144], in_=w["sd"][:, :]), inc=hw)
    tok_scl = (hw, hw.n)
    tsd2 = P.run("vector", lambda e: e.tensor_scalar_mul(
        scl[:, 144:160], scl[:, 128:144], half_scale), waits=[tok_scl], inc=dve)

    for c in range(NCH):
        for hh in range(2):
            tok_sx = P.run("tensor", lambda e, c=c, hh=hh: e.matmul(
                ps_row[:, hh * 512:hh * 512 + 512], b.ones_col_bf[:],
                xn[:, c * T + hh * 512: c * T + hh * 512 + 512],
                start=(c == 0), stop=(c == NCH - 1)),
                waits=[xn_toks[c]], inc=(pe if (c == NCH - 1 and hh == 1) else None))
    tok_sxe = P.run("vector", lambda e: e.tensor_copy(rows[0:1, 0:T], ps_row[:, :]),
                    waits=[tok_sx], inc=dve)

    def dma_gu(mat, g, bb, slot, wait):
        i0 = (g * NB + bb * NBATCH) * 128
        src = mat.rearrange("(c p) i -> p c i", p=128)[:, :, i0:i0 + GW]
        return P.run("gpsimd", lambda e, src=src, slot=slot: e.dma_start(
            out=wg[:, slot * NCH * GW:(slot + 1) * NCH * GW].rearrange(
                "p (c j) -> p c j", c=NCH),
            in_=src), waits=[wait], inc=sw)

    def dma_dn(g, hq, s2, slot, wait):
        c0 = g * NB + s2 * DNB
        src = w["downT"].rearrange("(c p) h -> p c h", p=128)[
            :, c0:c0 + DNB, hq * 512:(hq + 1) * 512]
        return P.run("gpsimd", lambda e, src=src, slot=slot: e.dma_start(
            out=wd[:, slot * DNB * 512:(slot + 1) * DNB * 512].rearrange(
                "p (c j) -> p c j", c=DNB),
            in_=src), waits=[wait], inc=sw)

    NBB = NB // NBATCH
    gu_seq = [(mt, g, bb) for g in range(NGRP) for bb in range(NBB) for mt in ("gu"[0], "gu"[1])]
    gu_seq = []
    for g in range(NGRP):
        for bb in range(NBB):
            gu_seq.append(("g", g, bb))
            gu_seq.append(("u", g, bb))
    gu_toks, gu_slot = {}, {}
    slot_free = [None, None]
    for idx in range(2):
        mt, g, bb = gu_seq[idx]
        gu_toks[(mt, g, bb)] = dma_gu(w["gateT"] if mt == "g" else w["upT"], g, bb,
                                      idx % 2, slot_free[idx % 2])
        gu_slot[(mt, g, bb)] = idx % 2
    gu_next = 2

    dn_seq = [(g, hq, s2) for g in range(NGRP) for hq in range(4) for s2 in range(2)]
    dn_toks, dn_slot = {}, {}
    dn_slot_free = [None, None]
    dn_slot_free[1] = None
    t_wdinit = P.run("vector", lambda e: e.memset(wd[:], 0.0), inc=dve)
    dn_toks[dn_seq[0]] = dma_dn(0, 0, 0, 0, t_wdinit)
    dn_slot[dn_seq[0]] = 0
    dn_next = 1

    ps_free = [tok_sxe, None, None, None]
    actb_free = [None] * NB
    sgb_free = [None, None]
    tok_act = None

    for g in range(NGRP):
        for bb in range(NBB):
            for mt in ("g", "u"):
                key = (mt, g, bb)
                tok_w = gu_toks.pop(key)
                slot = gu_slot.pop(key)
                base = slot * NCH * GW
                for hc in range(NCH):
                    for ib in range(NBATCH):
                        for hh in range(2):
                            wl = [tok_w, xn_done]
                            if hc == 0 and hh == 0:
                                wl.append(ps_free[ib])
                            P.run("tensor", lambda e, hc=hc, ib=ib, hh=hh, base=base: e.matmul(
                                ps[ib][:, hh * 512:hh * 512 + 512],
                                wg[:, base + hc * GW + ib * 128: base + hc * GW + ib * 128 + 128],
                                xn[:, hc * T + hh * 512: hc * T + hh * 512 + 512],
                                start=(hc == 0), stop=False),
                                waits=wl)
                for ib in range(NBATCH):
                    for hh in range(2):
                        tok_fix = P.run("tensor", lambda e, ib=ib, hh=hh: e.matmul(
                            ps[ib][:, hh * 512:hh * 512 + 512],
                            b.neg128_row[:],
                            rows[0:1, hh * 512:hh * 512 + 512],
                            start=False, stop=True),
                            waits=[tok_sxe],
                            inc=(pe if (ib == NBATCH - 1 and hh == 1) else None))
                slot_free[slot] = tok_fix
                if gu_next < len(gu_seq):
                    mt2, g2, bb2 = gu_seq[gu_next]
                    s2 = gu_next % 2
                    gu_toks[(mt2, g2, bb2)] = dma_gu(
                        w["gateT"] if mt2 == "g" else w["upT"], g2, bb2, s2, slot_free[s2])
                    gu_slot[(mt2, g2, bb2)] = s2
                    gu_next += 1
                ibase = bb * NBATCH
                if mt == "g":
                    for ib in range(NBATCH):
                        col = g * NB + ibase + ib
                        tok_sg = P.run("scalar", lambda e, ib=ib, col=col: e.activation(
                            sgb[:, ib * T:(ib + 1) * T], ps[ib][:, :], AF.Silu,
                            scale=scl[:, col:col + 1]),
                            waits=[tok_fix, tok_scl, sgb_free[ib]], inc=act)
                        ps_free[ib] = tok_sg
                else:
                    for ib in range(NBATCH):
                        col = 64 + g * NB + ibase + ib
                        tok_act = P.run("vector", lambda e, ib=ib, col=col, ibase=ibase: e.scalar_tensor_tensor(
                            actb[:, (ibase + ib) * T:(ibase + ib + 1) * T],
                            ps[ib][:, :], scl[:, col:col + 1], sgb[:, ib * T:(ib + 1) * T],
                            ALU.mult, ALU.mult),
                            waits=[tok_fix, tok_sg, actb_free[ibase + ib]], inc=dve)
                        ps_free[ib] = tok_act
                        sgb_free[ib] = tok_act
        for ib in range(NB):
            for hh in range(2):
                wl_sa = [tok_act]
                if ib == 0 and hh == 0:
                    wl_sa.append(ps_free[0])
                tok_sa = P.run("tensor", lambda e, ib=ib, hh=hh: e.matmul(
                    ps_row[:, hh * 512:hh * 512 + 512], b.ones_col_bf[:],
                    actb[:, ib * T + hh * 512: ib * T + hh * 512 + 512],
                    start=(ib == 0), stop=(ib == NB - 1)),
                    waits=wl_sa, inc=(pe if (ib == NB - 1 and hh == 1) else None))
        tok_sae = P.run("vector", lambda e: e.tensor_copy(rows[0:1, T:2 * T], ps_row[:, :]),
                        waits=[tok_sa], inc=dve)
        ps_free[0] = tok_sae
        for hq in range(4):
            for s2 in range(2):
                key = (g, hq, s2)
                tok_dw = dn_toks.pop(key)
                dslot = dn_slot.pop(key)
                dbase = dslot * DNB * 512
                for icl in range(DNB):
                    ic = s2 * DNB + icl
                    for hcq in range(4):
                        for hh in range(2):
                            wl = [tok_dw]
                            if ic == 0 and hh == 0:
                                wl.append(ps_free[hcq])
                                wl.append(tok_sae)
                            if ic == 0:
                                wl.append(tok_act)
                            is_last = (icl == DNB - 1 and hcq == 3 and hh == 1)
                            tok_dmm = P.run("tensor", lambda e, ic=ic, icl=icl, hcq=hcq, hh=hh, dbase=dbase, s2=s2: e.matmul(
                                ps[hcq][:, hh * 512:hh * 512 + 512],
                                wd[:, dbase + icl * 512 + hcq * 128: dbase + icl * 512 + hcq * 128 + 128],
                                actb[:, ic * T + hh * 512: ic * T + hh * 512 + 512],
                                start=(ic == 0), stop=False),
                                waits=wl, inc=(pe if is_last else None))
                dn_slot_free[dslot] = tok_dmm
                if dn_next < len(dn_seq):
                    g2, hq2, s22 = dn_seq[dn_next]
                    sl2 = dn_next % 2
                    dn_toks[(g2, hq2, s22)] = dma_dn(g2, hq2, s22, sl2, dn_slot_free[sl2])
                    dn_slot[(g2, hq2, s22)] = sl2
                    dn_next += 1
            for hcq in range(4):
                for hh in range(2):
                    tok_dfix = P.run("tensor", lambda e, hcq=hcq, hh=hh: e.matmul(
                        ps[hcq][:, hh * 512:hh * 512 + 512],
                        b.neg128_row[:],
                        rows[0:1, T + hh * 512: T + hh * 512 + 512],
                        start=False, stop=True),
                        waits=[tok_sae], inc=(pe if (hcq == 3 and hh == 1) else None))
            for hcq in range(4):
                hc = hq * 4 + hcq
                tok_ev = P.run("vector", lambda e, hcq=hcq, hc=hc: e.scalar_tensor_tensor(
                    x[:, hc * T:(hc + 1) * T], ps[hcq][:, :],
                    scl[:, 144 + hc:144 + hc + 1], x[:, hc * T:(hc + 1) * T],
                    ALU.mult, ALU.add),
                    waits=[tok_dfix, tsd2], inc=dve)
                ps_free[hcq] = tok_ev
        for ib in range(NB):
            actb_free[ib] = tok_dfix
    return tok_ev



def emit_proj(P, b, wst, wst_free, ppair, pp_free, xn, tok_ready, wap, out_w,
              evac, extra_mm_waits=()):
    sw, pe = P.ctxs["sw"], P.ctxs["pe"]
    nob = out_w // 128
    toks = {}

    def dma_w(ob, slot, wait):
        src = wap.rearrange("(c p) o -> p c o", p=128)[:, :, ob * 128:(ob + 1) * 128]
        return P.run("gpsimd", lambda e, src=src, slot=slot: e.dma_start(
            out=wst[:, slot * 2048:(slot + 1) * 2048].rearrange("p (c j) -> p c j", c=NCH),
            in_=src), waits=[wait], inc=sw)

    for idx in range(min(2, nob)):
        toks[idx] = dma_w(idx, idx % 2, wst_free[idx % 2])
    ev_toks = []
    for ob in range(nob):
        slot = ob % 2
        pt = ppair[ob % 2]
        tw = toks.pop(ob)
        base = slot * 2048
        for hc in range(NCH):
            for hh in range(2):
                wl = [tw, tok_ready] + list(extra_mm_waits)
                if hc == 0 and hh == 0:
                    wl.append(pp_free[ob % 2])
                tok_mm = P.run("tensor", lambda e, hc=hc, hh=hh, base=base, pt=pt: e.matmul(
                    pt[:, hh * 512:hh * 512 + 512],
                    wst[:, base + hc * 128: base + hc * 128 + 128],
                    xn[:, hc * T + hh * 512: hc * T + hh * 512 + 512],
                    start=(hc == 0), stop=(hc == NCH - 1)),
                    waits=wl, inc=(pe if (hc == NCH - 1 and hh == 1) else None))
        wst_free[slot] = tok_mm
        if ob + 2 < nob:
            toks[ob + 2] = dma_w(ob + 2, slot, wst_free[slot])
        tev = evac(ob, pt, tok_mm)
        pp_free[ob % 2] = tev
        ev_toks.append(tev)
    return ev_toks



SC0 = 1.0 / float(np.sqrt(128.0))


def emit_attn0(P, b, es, pspool, x, xn, xn_toks, w, cc_in, cc_out, groups=GROUPS, nqt=8, nh=16):
    nc = P.nc
    pe, act, dve, hw, sw, cc = (P.ctxs[k] for k in ("pe", "act", "dve", "hw", "sw", "cc"))
    xn_done = xn_toks[-1]
    p0, p1, p2 = pspool

    bt = [es(nc.psum_tensor(uname("p"), [128, 1024], BF16)) for i in range(2)]
    wst = es(nc.sbuf_tensor(uname("s"), [128, 2 * 2048], BF16))
    qf = es(nc.sbuf_tensor(uname("s"), [128, NCH * T], BF16))
    kfull = es(nc.sbuf_tensor(uname("s"), [128, 2048], BF16))
    vfull = es(nc.sbuf_tensor(uname("s"), [128, 2048], BF16))
    vtm = es(nc.sbuf_tensor(uname("s"), [128, 2048], BF16))
    pscr = es(nc.sbuf_tensor(uname("s"), [128, 4096], BF16))
    atT = es(nc.sbuf_tensor(uname("s"), [128, 2048], BF16))
    mrow = es(nc.sbuf_tensor(uname("s"), [1, 2048], BF16))
    diag = es(nc.sbuf_tensor(uname("s"), [128, 256], BF16))
    srows = es(nc.sbuf_tensor(uname("s"), [128, 8], F32))
    bocb = es(nc.sbuf_tensor(uname("s"), [128, NCH], F32))

    kself = pscr[:, 0:T]
    vself = pscr[:, T:2 * T]
    probs = [pscr[:, 0:2048], pscr[:, 2048:4096]]
    ao = xn
    if nqt < 8 or nh < 16:
        P.run("vector", lambda e: e.memset(pscr[:], 0.0))
        P.run("vector", lambda e: e.memset(atT[:], 0.0))
        P.run("vector", lambda e: e.memset(ao[:], 0.0))
        P.run("vector", lambda e: e.memset(srows[:], 0.0), inc=P.ctxs["dve"])

    tok_bo = P.run("sync", lambda e: e.dma_start(out=bocb[:], in_=w["bo_cb"][:, :]), inc=hw)

    wst_free = [None, None]
    pp_free = [None, None]

    def ev_q(ob, pt, tok_pe):
        return P.run("scalar", lambda e, ob=ob, pt=pt: e.activation(
            qf[:, ob * T:(ob + 1) * T], pt[:, :], AF.Copy, scale=SC0),
            waits=[tok_pe], inc=act)
    evq = emit_proj(P, b, wst, wst_free, [p0, p1], pp_free, xn, xn_done, w["wq"], D, ev_q)
    tok_qdone = evq[-1]

    def ev_k(ob, pt, tok_pe):
        return P.run("scalar", lambda e, pt=pt: e.activation(
            kself, pt[:, :], AF.Copy), waits=[tok_pe], inc=act)

    def ev_v(ob, pt, tok_pe):
        return P.run("scalar", lambda e, pt=pt: e.activation(
            vself, pt[:, :], AF.Copy), waits=[tok_pe], inc=act)
    tk = emit_proj(P, b, wst, wst_free, [p0, p1], pp_free, xn, xn_done, w["wk"], 128, ev_k)
    tv = emit_proj(P, b, wst, wst_free, [p0, p1], pp_free, xn, xn_done, w["wv"], 128, ev_v)

    t1 = P.run("sync", lambda e: e.dma_start(out=cc_in[0:128, :], in_=kself), waits=[tk[0]], inc=hw)
    t2 = P.run("sync", lambda e: e.dma_start(out=cc_in[128:256, :], in_=vself), waits=[tv[0]], inc=hw)
    tcc = P.run("gpsimd", lambda e, groups=groups: e.collective_compute(
        "AllGather", ALU.bypass, replica_groups=groups,
        ins=[cc_in.opt()], outs=[cc_out.opt()]),
        waits=[t1, t2], inc=cc, inc_k=1)
    P.run("sync", lambda e: e.dma_start(out=kfull[:, 0:T], in_=cc_out[0:128, :]), waits=[tcc], inc=hw)
    P.run("sync", lambda e: e.dma_start(out=kfull[:, T:2 * T], in_=cc_out[256:384, :]), inc=hw)
    P.run("sync", lambda e: e.dma_start(out=vfull[:, 0:T], in_=cc_out[128:256, :]), inc=hw)
    P.run("sync", lambda e: e.dma_start(out=vfull[:, T:2 * T], in_=cc_out[384:512, :]), inc=hw)
    tok_kf = (hw, hw.n)

    bt_free = [None, None]
    tok_vtm = None
    for half in range(2):
        bk = half
        for j in range(8):
            kc = half * 8 + j
            tok_tr = P.run("tensor", lambda e, j=j, kc=kc, bk=bk: e.transpose(
                bt[bk][:, j * 128:(j + 1) * 128], vfull[:, kc * 128:(kc + 1) * 128],
                b.ident_bf[:]), waits=[tok_kf, bt_free[bk]],
                inc=(pe if j == 7 else None))
        tok_vtm = P.run("vector", lambda e, half=half, bk=bk: e.tensor_copy(
            vtm[:, half * 1024:(half + 1) * 1024], bt[bk][:, :]),
            waits=[tok_tr], inc=dve)
        bt_free[bk] = tok_vtm

    mrow_free = diag_free = None
    scores_free = [pp_free[0], pp_free[1]]
    probs_free = [None, None]
    atT_free = None
    av_free = xn_done
    tok_ao = None
    tok_sc = None
    pending = None

    def emit_tail(qt, h, tok_pn):
        nonlocal atT_free, av_free, tok_ao, probs_free, bt_free
        nkc = 9 + qt
        slot = h % 2
        ev_toks = []
        tok_tr_last = None
        for half in range(ceil_div(nkc, 8)):
            bk = half % 2
            lo8 = half * 8
            n_in = min(8, nkc - lo8)
            for j in range(n_in):
                kc = lo8 + j
                tok_tr = P.run("tensor", lambda e, j=j, kc=kc, bk=bk, slot=slot: e.transpose(
                    bt[bk][:, j * 128:(j + 1) * 128], probs[slot][:, kc * 128:(kc + 1) * 128],
                    b.ident_bf[:]),
                    waits=[tok_pn, bt_free[bk]],
                    inc=(pe if j == n_in - 1 else None))
            tok_evc = P.run("vector", lambda e, bk=bk, lo8=lo8, n_in=n_in: e.tensor_copy(
                atT[:, lo8 * 128: lo8 * 128 + n_in * 128], bt[bk][:, 0:n_in * 128]),
                waits=[tok_tr, atT_free], inc=dve)
            bt_free[bk] = tok_evc
            ev_toks.append(tok_evc)
            tok_tr_last = tok_tr
        probs_free[slot] = tok_tr_last
        for kc in range(nkc):
            wl = [ev_toks[kc // 8], tok_vtm]
            if kc == 0:
                wl.append(av_free)
            tok_av = P.run("tensor", lambda e, kc=kc, nkc=nkc: e.matmul(
                p2[:, 0:128], vtm[:, kc * 128:(kc + 1) * 128], atT[:, kc * 128:(kc + 1) * 128],
                start=(kc == 0), stop=(kc == nkc - 1)),
                waits=wl, inc=(pe if kc == nkc - 1 else None))
        atT_free = tok_av
        tok_ao = P.run("scalar", lambda e, h=h, qt=qt: e.activation(
            ao[:, h * T + qt * 128: h * T + qt * 128 + 128], p2[:, 0:128], AF.Copy),
            waits=[tok_av], inc=act)
        av_free = tok_ao

    for qt in range(8):
        nkc = 9 + qt
        hi = (qt + 1) * 128
        tok_mr = P.run("sync", lambda e, qt=qt: e.dma_start(
            out=mrow[0:1, :], in_=w["maskrow"][qt:qt + 1, :]), waits=[mrow_free], inc=hw)
        tok_dg = P.run("sync", lambda e, qt=qt: e.dma_start(
            out=diag[:, :], in_=w["diag"][qt, :, :]), waits=[diag_free], inc=hw)
        for h in range(NCH):
            slot = h % 2
            q0 = h * T + qt * 128
            if hi > 512:
                sg_a = [(p0, 0, 512, 0), (p0, 512, 512, 512)]
                sg_b = [(p1, 0, 512, 1024), (p1, 512, hi - 512, 1536)]
            else:
                sg_a = [(p0, 0, 512, 0), (p0, 512, 512, 512)]
                sg_b = [(p1, 0, hi, 1024)]

            def emit_sgroups(sgroups):
                nonlocal tok_sc
                for gi, (pt, off, wdt, key0) in enumerate(sgroups):
                    cands = []
                    if key0 <= qt * 128 < key0 + wdt:
                        cands.append((0, qt * 128 - key0))
                    if key0 <= 1024 + qt * 128 < key0 + wdt:
                        cands.append((1, 1024 + qt * 128 - key0))
                    wl = [scores_free[0 if pt is p0 else 1], tok_mr, tok_dg, tok_kf, tok_qdone]
                    P.run("tensor", lambda e, pt=pt, off=off, wdt=wdt, key0=key0, q0=q0: e.matmul(
                        pt[:, off:off + wdt], qf[:, q0:q0 + 128], kfull[:, key0:key0 + wdt],
                        start=True, stop=False), waits=wl)
                    for (cand, cpos) in cands:
                        P.run("tensor", lambda e, pt=pt, off=off, cpos=cpos, cand=cand: e.matmul(
                            pt[:, off + cpos:off + cpos + 128], b.ident_bf[:],
                            diag[:, cand * 128:(cand + 1) * 128],
                            start=False, stop=False))
                    tok_sc = P.run("tensor", lambda e, pt=pt, off=off, wdt=wdt, key0=key0: e.matmul(
                        pt[:, off:off + wdt], b.ones_row_bf[0:1, 0:128], mrow[0:1, key0:key0 + wdt],
                        start=False, stop=True),
                        inc=(pe if gi == len(sgroups) - 1 else None))

            emit_sgroups(sg_a)
            tok_sc_p0 = tok_sc
            tr1 = P.run("vector", lambda e: e.reduce_max(srows[:, 0:1], p0[:, :], axis=AX.X, negate=True),
                        waits=[tok_sc_p0], inc=dve)
            emit_sgroups(sg_b)
            tr2 = P.run("vector", lambda e, hi=hi: e.reduce_max(srows[:, 1:2], p1[:, 0:hi], axis=AX.X, negate=True),
                        waits=[tok_sc, tr1], inc=dve)
            tok_rm = P.run("vector", lambda e: e.tensor_tensor(
                srows[:, 2:3], srows[:, 0:1], srows[:, 1:2], ALU.min), waits=[tr2], inc=dve)
            P.run("scalar", lambda e, slot=slot: e.activation(
                probs[slot][:, 0:1024], p0[:, :], AF.Exp, bias=srows[:, 2:3],
                accum_out=srows[:, 3:4]), waits=[tok_rm, probs_free[slot]])
            tok_exp = P.run("scalar", lambda e, slot=slot, hi=hi: e.activation(
                probs[slot][:, 1024:1024 + hi], p1[:, 0:hi], AF.Exp, bias=srows[:, 2:3],
                accum_out=srows[:, 4:5]), inc=act)
            scores_free[0] = tok_exp
            scores_free[1] = tok_exp
            ta1 = P.run("vector", lambda e: e.tensor_add(srows[:, 5:6], srows[:, 3:4], srows[:, 4:5]),
                        waits=[tok_exp], inc=dve)
            ta2 = P.run("vector", lambda e: e.reciprocal(srows[:, 6:7], srows[:, 5:6]),
                        waits=[ta1], inc=dve)
            tok_pn = P.run("vector", lambda e, slot=slot, hi=hi: e.tensor_scalar_mul(
                probs[slot][:, 0:1024 + hi], probs[slot][:, 0:1024 + hi], srows[:, 6:7]),
                waits=[ta2], inc=dve)
            if pending is not None:
                emit_tail(*pending)
            pending = (qt, h, tok_pn)
        mrow_free = tok_sc
        diag_free = tok_sc
    emit_tail(*pending)
    tok_ao_done = tok_ao

    pp_free2 = [scores_free[0], scores_free[1]]

    def ev_o(ob, pt, tok_pe):
        return P.run("vector", lambda e, ob=ob, pt=pt: e.scalar_tensor_tensor(
            x[:, ob * T:(ob + 1) * T], pt[:, :], bocb[:, ob:ob + 1], x[:, ob * T:(ob + 1) * T],
            ALU.add, ALU.add), waits=[tok_pe, tok_bo], inc=dve)
    ev = emit_proj(P, b, wst, wst_free, [p0, p1], pp_free2, ao, tok_ao_done, w["wo"], D, ev_o)
    P.dbg = {"qf": qf, "kfull": kfull, "vfull": vfull, "vtm": vtm, "ao": ao, "pscr": pscr, "atT": atT}
    P.dbg_srows = srows
    return ev[-1]



def emit_moe(P, b, es, pspool, x, xn, xn_toks, st, w, half_scale=0.5, pairs=(0, 1, 2, 3)):
    nc = P.nc
    pe, act, dve, hw, sw = (P.ctxs[k] for k in ("pe", "act", "dve", "hw", "sw"))
    xn_done = xn_toks[-1]
    p0, p1, p2, p3 = pspool
    NIB = I1 // 128
    DN2 = (6, 5)

    wg = es(nc.sbuf_tensor(uname("s"), [128, 2 * NCH * 256], BF16))
    wd = es(nc.sbuf_tensor(uname("s"), [128, 2 * 6 * 512], BF16))
    sgb = es(nc.sbuf_tensor(uname("s"), [128, 2 * T], BF16))
    actb = es(nc.sbuf_tensor(uname("s"), [128, 2 * NIB * T], BF16))
    rsb = es(nc.sbuf_tensor(uname("s"), [128, 128], F32))
    lsb = es(nc.sbuf_tensor(uname("s"), [128, 8], F32))
    msk = es(nc.sbuf_tensor(uname("s"), [128, 88], F32))
    mwbf = es(nc.sbuf_tensor(uname("s"), [128, 64], BF16))
    rcols = es(nc.sbuf_tensor(uname("s"), [128, 24], F32))
    rrow = es(nc.sbuf_tensor(uname("s"), [1, 16], F32))
    mrow_sb = es(nc.sbuf_tensor(uname("s"), [1, T], F32))

    tok_r = P.run("sync", lambda e: e.dma_start(out=rsb[:], in_=w["rcb"][:, :]), inc=hw)
    for hc in range(NCH):
        wl = [tok_r]
        if hc == 0:
            wl.append(xn_done)
        tok_rw = P.run("tensor", lambda e, hc=hc: e.matmul(
            p2[0:1, 0:8], b.ones_col_f32[:], rsb[:, hc * 8:(hc + 1) * 8],
            start=(hc == 0), stop=(hc == NCH - 1)),
            waits=wl, inc=(pe if hc == NCH - 1 else None))
    t_rw = P.run("vector", lambda e: e.tensor_copy(rrow[0:1, 0:8], p2[0:1, 0:8]),
                 waits=[tok_rw], inc=dve)
    t_nrw = P.run("vector", lambda e: e.tensor_scalar_mul(rrow[0:1, 8:16], rrow[0:1, 0:8], -1.0),
                  waits=[t_rw], inc=dve)
    for tt in range(8):
        wl = [st.tok_r]
        if tt == 0:
            wl.append(xn_done)
        P.run("tensor", lambda e, tt=tt: e.matmul(
            p3[:, tt:tt + 1], st.row_r[0:1, tt * 128:(tt + 1) * 128], b.one_one[0:1, 0:1],
            start=True, stop=True), waits=wl)
        tok_rc = P.run("tensor", lambda e, tt=tt: e.matmul(
            p3[:, 8 + tt:9 + tt], st.row_m[0:1, tt * 128:(tt + 1) * 128], b.one_one[0:1, 0:1],
            start=True, stop=True), waits=[st.tok_m], inc=(pe if tt == 7 else None))
    t_rc = P.run("vector", lambda e: e.tensor_copy(rcols[:, 0:16], p3[:, 0:16]),
                 waits=[tok_rc], inc=dve)
    tok_brw = P.run("tensor", lambda e: e.matmul(
        p3[:, 16:24], b.ones_row_f32[:], rrow[0:1, 0:8], start=True, stop=True),
        waits=[t_rw, t_rc], inc=pe)
    t_brw = P.run("vector", lambda e: e.tensor_copy(rcols[:, 16:24], p3[:, 16:24]),
                  waits=[tok_brw], inc=dve)
    t_mask_done = None
    t_lsb_free = t_rc
    p2_free = None
    for tt in range(8):
        for hc in range(NCH):
            wl = [t_rc, t_brw]
            if hc == 0:
                wl.append(p2_free)
            tok_lg = P.run("tensor", lambda e, tt=tt, hc=hc: e.matmul(
                p2[:, 0:8], x[:, hc * T + tt * 128: hc * T + tt * 128 + 128],
                rsb[:, hc * 8:(hc + 1) * 8],
                start=(hc == 0), stop=(hc == NCH - 1)), waits=wl,
                inc=(pe if hc == NCH - 1 else None))
        t_l0 = P.run("vector", lambda e: e.tensor_copy(lsb[:], p2[:, 0:8]),
                     waits=[tok_lg, t_lsb_free], inc=dve)
        p2_free = t_l0
        t_fx = P.run("vector", lambda e, tt=tt: e.tensor_scalar_mul(
            msk[:, 76:84], rcols[:, 16:24], rcols[:, 8 + tt:9 + tt]),
            waits=[t_brw, t_l0], inc=dve)
        t_l = P.run("vector", lambda e: e.tensor_sub(lsb[:], lsb[:], msk[:, 76:84]),
                    waits=[t_fx], inc=dve)
        t_m0 = P.run("vector", lambda e: e.reduce_max(msk[:, 84:85], lsb[:], axis=AX.X),
                     waits=[t_l], inc=dve)
        t_eq = P.run("vector", lambda e: e.tensor_single_scalar(
            msk[:, 72:80], lsb[:], msk[:, 84:85], ALU.is_equal), waits=[t_m0], inc=dve)
        t_t2 = P.run("vector", lambda e: e.scalar_tensor_tensor(
            msk[:, 64:72], msk[:, 72:80], -1e30, lsb[:], ALU.mult, ALU.add),
            waits=[t_eq], inc=dve)
        t_m1 = P.run("vector", lambda e: e.reduce_max(msk[:, 85:86], msk[:, 64:72], axis=AX.X),
                     waits=[t_t2], inc=dve)
        t_gap = P.run("vector", lambda e: e.tensor_sub(msk[:, 86:87], msk[:, 84:85], msk[:, 85:86]),
                      waits=[t_m1], inc=dve)
        t_w = P.run("scalar", lambda e, tt=tt: e.activation(
            msk[:, 87:88], msk[:, 86:87], AF.Sigmoid, scale=rcols[:, tt:tt + 1]),
            waits=[t_gap, t_rc], inc=act)
        t_mw = P.run("vector", lambda e, tt=tt: e.tensor_scalar_mul(
            msk[:, tt * 8:(tt + 1) * 8], msk[:, 72:80], msk[:, 87:88]),
            waits=[t_w], inc=dve)
        t_mask_done = P.run("vector", lambda e, tt=tt: e.tensor_copy(
            mwbf[:, tt * 8:(tt + 1) * 8], msk[:, tt * 8:(tt + 1) * 8]),
            waits=[t_mw], inc=dve)
        t_lsb_free = t_mask_done

    GW = 256
    NBB = 6
    bwidth = [256, 256, 256, 256, 256, 128]
    boff = [0, 256, 512, 768, 1024, 1280]

    def dma_gu(mat, e_, bb, slot, wait):
        i0 = boff[bb]
        bw = bwidth[bb]
        src = mat[e_].rearrange("(c p) i -> p c i", p=128)[:, :, i0:i0 + bw]
        return P.run("gpsimd", lambda e, src=src, slot=slot, bw=bw: e.dma_start(
            out=wg[:, slot * NCH * GW: slot * NCH * GW + NCH * bw].rearrange(
                "p (c j) -> p c j", c=NCH),
            in_=src), waits=[wait], inc=sw)

    def dma_dn(e_, hq, s2, slot, wait):
        c0 = 0 if s2 == 0 else DN2[0]
        ncs = DN2[s2]
        src = w["downT"][e_].rearrange("(c p) h -> p c h", p=128)[
            :, c0:c0 + ncs, hq * 512:(hq + 1) * 512]
        return P.run("gpsimd", lambda e, src=src, slot=slot, ncs=ncs: e.dma_start(
            out=wd[:, slot * 6 * 512: slot * 6 * 512 + ncs * 512].rearrange(
                "p (c j) -> p c j", c=ncs),
            in_=src), waits=[wait], inc=sw)

    gu_seq = []
    for e_ in range(NE):
        for bb in range(NBB):
            gu_seq.append(("g", e_, bb))
            gu_seq.append(("u", e_, bb))
    gu_toks, gu_slot = {}, {}
    slot_free = [None, None]
    for idx in range(2):
        mt, e_, bb = gu_seq[idx]
        gu_toks[(mt, e_, bb)] = dma_gu(w["gateT"] if mt == "g" else w["upT"], e_, bb,
                                       idx % 2, slot_free[idx % 2])
        gu_slot[(mt, e_, bb)] = idx % 2
    gu_next = 2

    dn_seq = [(pr_ * 2 + ei_, hq_, s2_) for pr_ in range(4) for hq_ in range(4)
              for ei_ in range(2) for s2_ in range(2)]
    dn_toks, dn_slot = {}, {}
    dn_slot_free = [None, None]
    dn_slot_free[1] = None
    t_wdinit = P.run("vector", lambda e: e.memset(wd[:], 0.0), inc=dve)
    dn_toks[dn_seq[0]] = dma_dn(0, 0, 0, 0, t_wdinit)
    dn_slot[dn_seq[0]] = 0
    dn_next = 1

    ps_free = [None, None, None, None]
    ps_free[2] = p2_free
    ps_free[3] = t_rc
    actb_free = [None, None]
    sgb_free = [None, None]
    tok_ev = None

    for pr in range(4):
        if pr not in pairs:
            pass
        acts_done = []
        bm_toks = []
        for ei in range(2):
            e_ = pr * 2 + ei

            def emit_bmask(e_=e_):
                for tt in range(8):
                    wl = [t_mask_done, ps_free[2]]
                    P.run("tensor", lambda e, e_=e_, tt=tt: e.matmul(
                        p2[0:1, tt * 128:(tt + 1) * 128], mwbf[:, tt * 8 + e_:tt * 8 + e_ + 1],
                        b.ident_bf[:], start=True, stop=True),
                        waits=wl, inc=(pe if tt == 7 else None))
                tok_mre = P.run("vector", lambda e: e.tensor_copy(mrow_sb[0:1, :], p2[0:1, :]),
                                waits=[(pe, pe.n)], inc=dve)
                for hh in range(2):
                    tok_bm = P.run("tensor", lambda e, hh=hh: e.matmul(
                        p2[:, hh * 512:hh * 512 + 512], b.ones_row_f32[:],
                        mrow_sb[0:1, hh * 512:hh * 512 + 512], start=True, stop=True),
                        waits=[tok_mre], inc=(pe if hh == 1 else None))
                return tok_bm

            tok_bm = None
            tok_act = None
            for bb in range(NBB):
                nblk = bwidth[bb] // 128
                for mt in ("g", "u"):
                    key = (mt, e_, bb)
                    tw = gu_toks.pop(key)
                    slot = gu_slot.pop(key)
                    base = slot * NCH * GW
                    bw = bwidth[bb]
                    for hc in range(NCH):
                        for ib in range(nblk):
                            for hh in range(2):
                                wl = [tw, xn_done]
                                if hc == 0 and hh == 0:
                                    wl.append(ps_free[ib])
                                tok_mm = P.run("tensor", lambda e, hc=hc, ib=ib, hh=hh, base=base, bw=bw: e.matmul(
                                    pspool[ib][:, hh * 512:hh * 512 + 512],
                                    wg[:, base + hc * bw + ib * 128: base + hc * bw + ib * 128 + 128],
                                    xn[:, hc * T + hh * 512: hc * T + hh * 512 + 512],
                                    start=(hc == 0), stop=(hc == NCH - 1)),
                                    waits=wl,
                                    inc=(pe if (hc == NCH - 1 and ib == nblk - 1 and hh == 1) else None))
                    slot_free[slot] = tok_mm
                    if gu_next < len(gu_seq):
                        mt2, e2, bb2 = gu_seq[gu_next]
                        s2 = gu_next % 2
                        gu_toks[(mt2, e2, bb2)] = dma_gu(
                            w["gateT"] if mt2 == "g" else w["upT"], e2, bb2, s2, slot_free[s2])
                        gu_slot[(mt2, e2, bb2)] = s2
                        gu_next += 1
                    if bb == 0 and mt == "g" and tok_bm is None:
                        tok_bm = emit_bmask()
                    ibase = boff[bb] // 128
                    if mt == "g":
                        for ib in range(nblk):
                            tok_sg = P.run("scalar", lambda e, ib=ib, ibase=ibase: e.activation(
                                sgb[:, ib * T:(ib + 1) * T], pspool[ib][:, :], AF.Silu),
                                waits=[tok_mm, sgb_free[ib]], inc=act)
                            ps_free[ib] = tok_sg
                    else:
                        for ib in range(nblk):
                            tok_act = P.run("vector", lambda e, ib=ib, ibase=ibase, ei=ei: e.scalar_tensor_tensor(
                                actb[:, (ei * NIB + ibase + ib) * T:(ei * NIB + ibase + ib + 1) * T],
                                pspool[ib][:, :], 1.0, sgb[:, ib * T:(ib + 1) * T],
                                ALU.mult, ALU.mult),
                                waits=[tok_mm, tok_sg, actb_free[ei]], inc=dve)
                            ps_free[ib] = tok_act
                            sgb_free[ib] = tok_act
            t_am = None
            for ic in range(NIB):
                t_am = P.run("vector", lambda e, ic=ic, ei=ei: e.tensor_mul(
                    actb[:, (ei * NIB + ic) * T:(ei * NIB + ic + 1) * T],
                    actb[:, (ei * NIB + ic) * T:(ei * NIB + ic + 1) * T],
                    p2[:, :]), waits=[tok_act, tok_bm], inc=dve)
            acts_done.append(t_am)
            ps_free[2] = t_am
        tok_acts = acts_done[-1]
        for hq in range(4):
            tok_dmm = None
            for ei in range(2):
                e_ = pr * 2 + ei
                for s2 in range(2):
                    key = (e_, hq, s2)
                    tw = dn_toks.pop(key)
                    dslot = dn_slot.pop(key)
                    dbase = dslot * 6 * 512
                    c0 = 0 if s2 == 0 else DN2[0]
                    ncs = DN2[s2]
                    for icl in range(ncs):
                        ic = c0 + icl
                        first = (ei == 0 and ic == 0)
                        for hcq in range(4):
                            for hh in range(2):
                                wl = [tw, acts_done[ei]]
                                if first and hh == 0:
                                    wl.append(ps_free[hcq])
                                last_w = (ei == 1 and s2 == 1 and icl == ncs - 1)
                                tile_last = (icl == ncs - 1 and hcq == 3 and hh == 1)
                                tok_dmm = P.run("tensor", lambda e, icl=icl, ic=ic, hcq=hcq, hh=hh, dbase=dbase, ei=ei, first=first, last_w=last_w: e.matmul(
                                    pspool[hcq][:, hh * 512:hh * 512 + 512],
                                    wd[:, dbase + icl * 512 + hcq * 128: dbase + icl * 512 + hcq * 128 + 128],
                                    actb[:, (ei * NIB + ic) * T + hh * 512: (ei * NIB + ic) * T + hh * 512 + 512],
                                    start=first, stop=last_w),
                                    waits=wl, inc=(pe if tile_last else None))
                    dn_slot_free[dslot] = tok_dmm
                    if dn_next < len(dn_seq):
                        e2, hq2, s22 = dn_seq[dn_next]
                        sl2 = dn_next % 2
                        dn_toks[(e2, hq2, s22)] = dma_dn(e2, hq2, s22, sl2, dn_slot_free[sl2])
                        dn_slot[(e2, hq2, s22)] = sl2
                        dn_next += 1
            for hcq in range(4):
                hc = hq * 4 + hcq
                tok_ev = P.run("vector", lambda e, hcq=hcq, hc=hc: e.scalar_tensor_tensor(
                    x[:, hc * T:(hc + 1) * T], pspool[hcq][:, :],
                    half_scale, x[:, hc * T:(hc + 1) * T],
                    ALU.mult, ALU.add),
                    waits=[tok_dmm], inc=dve)
                ps_free[hcq] = tok_ev
        actb_free = [tok_dmm, tok_dmm]
    P.dbg_mw = mwbf
    P.dbg_actb = actb
    P.dbg_wd = wd
    return tok_ev


def moe_weight_arrays(inputs):
    rw = np.asarray(inputs["L1_ffn_router_w"], np.float32)
    rcb = np.empty((128, 128), np.float32)
    for hcc in range(NCH):
        rcb[:, hcc * 8:(hcc + 1) * 8] = rw[:, hcc * 128:(hcc + 1) * 128].T
    gateT = np.ascontiguousarray(np.asarray(inputs["L1_ffn_gate_w"], np.float32).transpose(0, 2, 1))
    upT = np.ascontiguousarray(np.asarray(inputs["L1_ffn_up_w"], np.float32).transpose(0, 2, 1))
    downT = np.ascontiguousarray(np.asarray(inputs["L1_ffn_down_w"], np.float32).transpose(0, 2, 1))
    return {"rcb": np.ascontiguousarray(rcb), "gateT": gateT, "upT": upT, "downT": downT}


def declare_moe_weights(nc):
    w = {}
    w["rcb"] = nc.dram_tensor("w_rcb", [128, 128], F32, kind="ExternalInput").ap()
    w["gateT"] = nc.dram_tensor("w_m_gateT", [NE, D, I1], F32, kind="ExternalInput").ap()
    w["upT"] = nc.dram_tensor("w_m_upT", [NE, D, I1], F32, kind="ExternalInput").ap()
    w["downT"] = nc.dram_tensor("w_m_downT", [NE, I1, D], F32, kind="ExternalInput").ap()
    return w


def build_moe_launch(debug=False):
    from contextlib import ExitStack
    nc = bass.Bass("TRN2", target_bir_lowering=False, detect_race_conditions=False)
    xin = nc.dram_tensor("xin", [128, NCH * T], F32, kind="ExternalInput").ap()
    xout = nc.dram_tensor("xout", [128, NCH * T], F32, kind="ExternalOutput").ap()
    ident = nc.dram_tensor("w_ident", [128, 128], BF16, kind="ExternalInput").ap()
    w = declare_moe_weights(nc)
    with ExitStack() as stack:
        es = stack.enter_context
        x = es(nc.sbuf_tensor("x", [128, NCH * T], F32))
        xn = es(nc.sbuf_tensor("xn", [128, NCH * T], BF16))
        b = alloc_consts(nc, es)
        b.ident_bf = es(nc.sbuf_tensor("ident_bf", [128, 128], BF16))
        ctxs = make_ctxs(nc, es)
        P = Prog(nc, ctxs)
        hw = ctxs["hw"]
        setup_consts(P, b)
        tok_x = P.run("sync", lambda e: e.dma_start(out=x[:], in_=xin[:, :]), inc=hw)
        P.run("sync", lambda e: e.dma_start(out=b.ident_bf[:], in_=ident[:, :]), inc=hw)
        pspool = [es(nc.psum_tensor(f"pspool{i}", [128, T], F32)) for i in range(4)]
        st = alloc_ln_sbuf(nc, es)
        ln_psum_views(st, pspool[2], pspool[3])
        xn_toks, _ = emit_ln(P, b, st, x[:], xn[:], tok_x)
        tok = emit_moe(P, b, es, pspool, x[:], xn[:], xn_toks, st, w)
        alltoks = [tok, (P.ctxs["dve"], P.ctxs["dve"].n), (P.ctxs["act"], P.ctxs["act"].n),
                   (P.ctxs["pe"], P.ctxs["pe"].n)]
        tst = P.run("sync", lambda e: e.dma_start(out=xout[:, :], in_=x[:]),
                    waits=alltoks, inc=hw)
        if debug:
            dm = nc.dram_tensor("dbg_mask", [128, 64], BF16, kind="ExternalOutput").ap()
            tst = P.run("sync", lambda e, dm=dm: e.dma_start(out=dm[:, :], in_=P.dbg_mw[:]), inc=hw)
            dl = nc.dram_tensor("dbg_acts", [128, 2 * 11 * T], BF16, kind="ExternalOutput").ap()
            tst = P.run("sync", lambda e, dl=dl: e.dma_start(out=dl[:, :], in_=P.dbg_actb[:]), inc=hw)
            dwd = nc.dram_tensor("dbg_wd", [128, 2 * 6 * 512], BF16, kind="ExternalOutput").ap()
            tst = P.run("sync", lambda e, dwd=dwd: e.dma_start(out=dwd[:, :], in_=P.dbg_wd[:]), inc=hw)
        P.run("sync", lambda e: e.nop(), waits=[tst])
        P.emit()
    return nc



def emit_attn1(P, b, es, pspool, x, xn, xn_toks, st, w, qsc, cc_in, cc_out,
               groups=GROUPS):
    nc = P.nc
    pe, act, dve, hw, sw, cc = (P.ctxs[k] for k in ("pe", "act", "dve", "hw", "sw", "cc"))
    xn_done = xn_toks[-1]
    p0, p1, p2 = pspool

    bt = [es(nc.psum_tensor(uname("p"), [128, 1024], BF16)) for i in range(2)]
    wq_st = es(nc.sbuf_tensor(uname("s"), [128, 2 * 2048], BF16))
    wv_st = es(nc.sbuf_tensor(uname("s"), [128, 2 * 2048], BF16))
    uf = es(nc.sbuf_tensor(uname("s"), [128, NCH * T], BF16))
    qh = es(nc.sbuf_tensor(uname("s"), [128, 2 * T], BF16))
    vh = es(nc.sbuf_tensor(uname("s"), [128, 2 * T], BF16))
    esc = es(nc.sbuf_tensor(uname("s"), [128, 2 * T], BF16))
    tsc = es(nc.sbuf_tensor(uname("s"), [128, 256], BF16))
    kvp = es(nc.sbuf_tensor(uname("s"), [128, 2048], BF16))
    kvq = es(nc.sbuf_tensor(uname("s"), [128, 2048], BF16))
    bias = es(nc.sbuf_tensor(uname("s"), [128, 6 * NCH], F32))
    ao = xn

    tok_b = P.run("sync", lambda e: e.dma_start(out=bias[:], in_=w["bias6"][:, :]), inc=hw)

    def dma_wt(wap, dst, ob, slot, wait):
        src = wap.rearrange("(c p) o -> p c o", p=128)[:, :, ob * 128:(ob + 1) * 128]
        return P.run("gpsimd", lambda e, src=src, slot=slot, dst=dst: e.dma_start(
            out=dst[:, slot * 2048:(slot + 1) * 2048].rearrange("p (c j) -> p c j", c=NCH),
            in_=src), waits=[wait], inc=sw)

    wq_free = [None, None]
    wv_free = [None, None]
    wq_toks = {0: dma_wt(w["wqk"], wq_st, 0, 0, None), 1: dma_wt(w["wqk"], wq_st, 1, 1, None)}
    wv_toks = {0: dma_wt(w["wv"], wv_st, 0, 0, None), 1: dma_wt(w["wv"], wv_st, 1, 1, None)}
    pq_free = None
    pv_free = None
    qh_free = [None, None]
    vh_free = [None, None]
    bt_free = [None, None]
    tsc_free = None
    kv_group_free = [None, None]
    kv_ev = [None, None]
    for h in range(NCH):
        slot = h % 2
        twq = wq_toks.pop(h)
        for hc in range(NCH):
            for hh in range(2):
                wl = [twq, xn_done, tok_b]
                if hc == 0 and hh == 0:
                    wl.append(pq_free)
                tok_qmm = P.run("tensor", lambda e, hc=hc, hh=hh, slot=slot: e.matmul(
                    p0[:, hh * 512:hh * 512 + 512],
                    wq_st[:, slot * 2048 + hc * 128: slot * 2048 + hc * 128 + 128],
                    xn[:, hc * T + hh * 512: hc * T + hh * 512 + 512],
                    start=(hc == 0), stop=(hc == NCH - 1)),
                    waits=wl, inc=(pe if (hc == NCH - 1 and hh == 1) else None))
        wq_free[slot] = tok_qmm
        if h + 2 < NCH:
            wq_toks[h + 2] = dma_wt(w["wqk"], wq_st, h + 2, slot, wq_free[slot])
        t_e1 = P.run("vector", lambda e, slot=slot, h=h: e.tensor_scalar(
            esc[:, slot * T:(slot + 1) * T], p0[:, :], bias[:, h:h + 1], 0.0,
            ALU.add, ALU.min), waits=[tok_qmm, qh_free[slot]], inc=dve)
        t_e3 = P.run("vector", lambda e, slot=slot, h=h: e.tensor_scalar(
            qh[:, slot * T:(slot + 1) * T], p0[:, :], bias[:, h:h + 1], 0.0,
            ALU.add, ALU.max), waits=[t_e1], inc=dve)
        pq_free = t_e3
        t_e2 = P.run("scalar", lambda e, slot=slot: e.activation(
            esc[:, slot * T:(slot + 1) * T], esc[:, slot * T:(slot + 1) * T], AF.Exp),
            waits=[t_e1], inc=act)
        t_q = P.run("vector", lambda e, slot=slot: e.scalar_tensor_tensor(
            qh[:, slot * T:(slot + 1) * T], esc[:, slot * T:(slot + 1) * T], -1.0,
            qh[:, slot * T:(slot + 1) * T], ALU.add, ALU.add),
            waits=[t_e2, t_e3], inc=dve)
        t_qs = P.run("sync", lambda e, h=h, slot=slot: e.dma_start(
            out=qsc[:, h * T:(h + 1) * T], in_=qh[:, slot * T:(slot + 1) * T]),
            waits=[t_q], inc=hw)
        twv = wv_toks.pop(h)
        for hc in range(NCH):
            for hh in range(2):
                wl = [twv, xn_done, tok_b]
                if hc == 0 and hh == 0:
                    wl.append(pv_free)
                tok_vmm = P.run("tensor", lambda e, hc=hc, hh=hh, slot=slot: e.matmul(
                    p1[:, hh * 512:hh * 512 + 512],
                    wv_st[:, slot * 2048 + hc * 128: slot * 2048 + hc * 128 + 128],
                    xn[:, hc * T + hh * 512: hc * T + hh * 512 + 512],
                    start=(hc == 0), stop=(hc == NCH - 1)),
                    waits=wl, inc=(pe if (hc == NCH - 1 and hh == 1) else None))
        wv_free[slot] = tok_vmm
        if h + 2 < NCH:
            wv_toks[h + 2] = dma_wt(w["wv"], wv_st, h + 2, slot, wv_free[slot])
        t_v = P.run("scalar", lambda e, slot=slot, h=h: e.activation(
            vh[:, slot * T:(slot + 1) * T], p1[:, :], AF.Silu,
            bias=bias[:, 1 * NCH + h:1 * NCH + h + 1]),
            waits=[tok_vmm, vh_free[slot]], inc=act)
        pv_free = t_v
        kvreg = p2[:, (h % 8) * 128:(h % 8) * 128 + 128]
        bank = (h % 8) // 4
        for tc in range(8):
            bk = tc % 2
            P.run("tensor", lambda e, tc=tc, bk=bk, slot=slot: e.transpose(
                bt[bk][:, 0:128], qh[:, slot * T + tc * 128: slot * T + tc * 128 + 128],
                b.ident_bf[:]), waits=[t_q, t_v, bt_free[bk]])
            tok_tr = P.run("tensor", lambda e, tc=tc, bk=bk, slot=slot: e.transpose(
                bt[bk][:, 128:256], vh[:, slot * T + tc * 128: slot * T + tc * 128 + 128],
                b.ident_bf[:]), inc=pe)
            t_ts = P.run("vector", lambda e, bk=bk: e.tensor_copy(
                tsc[:, 0:256], bt[bk][:, 0:256]),
                waits=[tok_tr, tsc_free], inc=dve)
            bt_free[bk] = t_ts
            wl = [t_ts]
            if tc == 0:
                wl.append(kv_group_free[bank])
            tok_kv = P.run("tensor", lambda e, tc=tc, kvreg=kvreg: e.matmul(
                kvreg, tsc[:, 0:128], tsc[:, 128:256],
                start=(tc == 0), stop=(tc == 7)),
                waits=wl, inc=(pe if tc == 7 else None))
            tsc_free = tok_kv
        qh_free[slot] = tok_kv
        vh_free[slot] = tok_kv
        if h % 4 == 3:
            t_kve = P.run("vector", lambda e, h=h: e.tensor_copy(
                kvp[:, (h - 3) * 128:(h + 1) * 128],
                p2[:, ((h % 8) - 3) * 128:((h % 8) + 1) * 128]),
                waits=[tok_kv], inc=dve)
            kv_group_free[bank] = t_kve
            kv_ev[bank] = t_kve
    tok_kvp = kv_ev[1]

    t1 = P.run("sync", lambda e: e.dma_start(out=cc_in[:, :], in_=kvp[:]),
               waits=[kv_ev[0], kv_ev[1]], inc=hw)
    tcc = P.run("gpsimd", lambda e, groups=groups: e.collective_compute(
        "AllGather", ALU.bypass, replica_groups=groups,
        ins=[cc_in.opt()], outs=[cc_out.opt()]), waits=[t1], inc=cc, inc_k=1)
    t2a = P.run("sync", lambda e: e.dma_start(out=kvq[:, :], in_=cc_out[0:128, :]),
                waits=[tcc], inc=hw)
    t2b = P.run("sync", lambda e: e.dma_start(out=kvp[:, :], in_=cc_out[128:256, :]),
                inc=hw)
    tok_kx = (hw, hw.n)
    t_kv = P.run("vector", lambda e: e.tensor_add(kvp[:], kvp[:], kvq[:]),
                 waits=[tok_kx], inc=dve)

    wst_free = [wq_free[0], wq_free[1]]
    pp_free = [pq_free, pv_free]

    def ev_u(ob, pt, tok_pe):
        return P.run("scalar", lambda e, ob=ob, pt=pt: e.activation(
            uf[:, ob * T:(ob + 1) * T], pt[:, :], AF.Silu,
            bias=bias[:, 2 * NCH + ob:2 * NCH + ob + 1]),
            waits=[tok_pe, tok_b], inc=act)
    evu = emit_proj(P, b, wq_st, wst_free, [p0, p1], pp_free, xn, xn_done, w["wu"], D, ev_u)
    tok_u = evu[-1]

    qb_toks = {}
    for h in range(2):
        qb_toks[h] = P.run("sync", lambda e, h=h: e.dma_start(
            out=qh[:, (h % 2) * T:(h % 2) * T + T], in_=qsc[:, h * T:(h + 1) * T]),
            waits=[qh_free[h % 2], tok_u], inc=hw)
    ao_toks = []
    for h in range(NCH):
        slot = h % 2
        tq = qb_toks.pop(h)
        for hh in range(2):
            wl = [tq, t_kv]
            if hh == 0:
                wl.append(pp_free[slot])
            tok_o = P.run("tensor", lambda e, h=h, hh=hh, slot=slot: e.matmul(
                (p0 if slot == 0 else p1)[:, hh * 512:hh * 512 + 512],
                kvp[:, h * 128:(h + 1) * 128],
                qh[:, slot * T + hh * 512: slot * T + hh * 512 + 512],
                start=True, stop=True),
                waits=wl, inc=(pe if hh == 1 else None))
        t_ao = P.run("scalar", lambda e, h=h, slot=slot: e.activation(
            ao[:, h * T:(h + 1) * T], (p0 if slot == 0 else p1)[:, :], AF.Copy),
            waits=[tok_o], inc=act)
        pp_free[slot] = t_ao
        qh_free[slot] = t_ao
        if h + 2 < NCH:
            qb_toks[h + 2] = P.run("sync", lambda e, h=h, slot=slot: e.dma_start(
                out=qh[:, slot * T:(slot + 1) * T], in_=qsc[:, (h + 2) * T:(h + 3) * T]),
                waits=[qh_free[slot]], inc=hw)
        ao_toks.append(t_ao)
    tok_ao = ao_toks[-1]

    st.cbuf_free = [None, None]
    st.qbuf_free = [None, None]
    for c in range(NCH):
        t2q = P.run("scalar", lambda e, c=c: e.activation(
            st.qbuf[:, (c % 2) * T:(c % 2) * T + T], ao[:, c * T:(c + 1) * T], AF.Square),
            waits=[ao_toks[c], st.qbuf_free[c % 2]], inc=act)
        for hh in range(2):
            wl_s1 = [ao_toks[c]]
            if c == 0 and hh == 0:
                wl_s1 += [tok_ao, t_kv]
            P.run("tensor", lambda e, c=c, hh=hh: e.matmul(
                st.ps_s1[:, hh * 512:hh * 512 + 512], b.ones_col_bf[:],
                ao[:, c * T + hh * 512: c * T + hh * 512 + 512],
                start=(c == 0), stop=(c == NCH - 1)), waits=wl_s1)
            tok_mm = P.run("tensor", lambda e, c=c, hh=hh: e.matmul(
                st.ps_s2[:, hh * 512:hh * 512 + 512], b.ones_col_bf[:],
                st.qbuf[:, (c % 2) * T + hh * 512:(c % 2) * T + hh * 512 + 512],
                start=(c == 0), stop=(c == NCH - 1)),
                waits=[t2q], inc=pe)
        st.qbuf_free[c % 2] = tok_mm
    tm = P.run("vector", lambda e: e.tensor_scalar_mul(st.row_m[:], st.ps_s1[:, :], 1.0 / D),
               waits=[tok_mm], inc=dve)
    te = P.run("vector", lambda e: e.tensor_scalar_mul(st.row_a[:], st.ps_s2[:, :], 1.0 / D),
               waits=[tm], inc=dve)
    tq_ = P.run("scalar", lambda e: e.activation(st.row_b[:], st.row_m[:], AF.Square),
                waits=[tm], inc=act)
    tv_ = P.run("vector", lambda e: e.tensor_sub(st.row_a[:], st.row_a[:], st.row_b[:]),
                waits=[te, tq_], inc=dve)
    ts_ = P.run("scalar", lambda e: e.activation(st.row_b[:], st.row_a[:], AF.Sqrt, bias=b.eps[0:1, 0:1]),
                waits=[tv_], inc=act)
    tr_ = P.run("vector", lambda e: e.reciprocal(st.row_r[:], st.row_b[:]),
                waits=[ts_], inc=dve)
    tmr = P.run("vector", lambda e: e.tensor_mul(st.row_mr[:], st.row_m[:], st.row_r[:]),
                waits=[tr_], inc=dve)
    for hh in range(2):
        osl = slice(hh * 512, hh * 512 + 512)
        P.run("tensor", lambda e, osl=osl: e.matmul(
            st.ps_br[:][:, osl], b.ones_row_f32[:], st.row_r[0:1, osl],
            start=True, stop=True), waits=[tr_])
        tb = P.run("tensor", lambda e, osl=osl: e.matmul(
            st.ps_bmr[:][:, osl], b.ones_row_f32[:], st.row_mr[0:1, osl],
            start=True, stop=True), waits=[tmr], inc=pe)
    t_m_last = None
    for c in range(NCH):
        P.run("vector", lambda e, c=c: e.tensor_mul(
            st.cbuf[:, (c % 2) * T:(c % 2) * T + T], ao[:, c * T:(c + 1) * T], st.ps_br[:][:, :]),
            waits=[tb, st.cbuf_free[c % 2]])
        t_c2 = P.run("vector", lambda e, c=c: e.tensor_sub(
            st.cbuf[:, (c % 2) * T:(c % 2) * T + T], st.cbuf[:, (c % 2) * T:(c % 2) * T + T],
            st.ps_bmr[:][:, :]), inc=dve)
        t_c3 = P.run("vector", lambda e, c=c: e.tensor_scalar(
            st.qbuf[:, (c % 2) * T:(c % 2) * T + T], st.cbuf[:, (c % 2) * T:(c % 2) * T + T],
            bias[:, 4 * NCH + c:4 * NCH + c + 1], bias[:, 5 * NCH + c:5 * NCH + c + 1],
            ALU.mult, ALU.add),
            waits=[t_c2, st.qbuf_free[c % 2], tok_b], inc=dve)
        t_m_last = P.run("vector", lambda e, c=c: e.tensor_mul(
            uf[:, c * T:(c + 1) * T], uf[:, c * T:(c + 1) * T],
            st.qbuf[:, (c % 2) * T:(c % 2) * T + T]),
            waits=[t_c3, tok_u], inc=dve)
        st.cbuf_free[c % 2] = t_m_last
        st.qbuf_free[c % 2] = t_m_last
    tok_m = t_m_last

    def ev_o(ob, pt, tok_pe):
        return P.run("vector", lambda e, ob=ob, pt=pt: e.scalar_tensor_tensor(
            x[:, ob * T:(ob + 1) * T], pt[:, :], bias[:, 3 * NCH + ob:3 * NCH + ob + 1],
            x[:, ob * T:(ob + 1) * T], ALU.add, ALU.add),
            waits=[tok_pe, tok_b], inc=dve)
    ev = emit_proj(P, b, wq_st, wst_free, [p0, p1], pp_free, uf, tok_m, w["wo"], D, ev_o)
    return ev[-1]


def attn1_weight_arrays(inputs):
    wqk = np.ascontiguousarray(inputs["L1_attn_Wqk"], dtype=np.float32)
    wv = np.ascontiguousarray(inputs["L1_attn_Wv"], dtype=np.float32)
    wu = np.ascontiguousarray(inputs["L1_attn_Wu"], dtype=np.float32)
    wo = np.ascontiguousarray(inputs["L1_attn_Wo"], dtype=np.float32)
    b6 = np.zeros((128, 6 * NCH), np.float32)
    for i, nm in enumerate(["L1_attn_bqk", "L1_attn_bv", "L1_attn_bu", "L1_attn_bo",
                            "L1_attn_ln_g", "L1_attn_ln_b"]):
        b6[:, i * NCH:(i + 1) * NCH] = np.asarray(inputs[nm], np.float32).reshape(NCH, 128).T
    return {"wqk": wqk, "wv": wv, "wu": wu, "wo": wo, "bias6": np.ascontiguousarray(b6)}


def declare_attn1_weights(nc):
    w = {}
    w["wqk"] = nc.dram_tensor("w_wqk", [D, D], F32, kind="ExternalInput").ap()
    w["wv"] = nc.dram_tensor("w_wv1", [D, D], F32, kind="ExternalInput").ap()
    w["wu"] = nc.dram_tensor("w_wu1", [D, D], F32, kind="ExternalInput").ap()
    w["wo"] = nc.dram_tensor("w_wo1", [D, D], F32, kind="ExternalInput").ap()
    w["bias6"] = nc.dram_tensor("w_bias6", [128, 6 * NCH], F32, kind="ExternalInput").ap()
    return w


def build_attn1_launch(groups=GROUPS):
    from contextlib import ExitStack
    nc = bass.Bass("TRN2", target_bir_lowering=False, detect_race_conditions=False)
    xin = nc.dram_tensor("xin", [128, NCH * T], F32, kind="ExternalInput").ap()
    xout = nc.dram_tensor("xout", [128, NCH * T], F32, kind="ExternalOutput").ap()
    ident = nc.dram_tensor("w_ident", [128, 128], BF16, kind="ExternalInput").ap()
    qsc = nc.dram_tensor("q_scratch", [128, NCH * T], BF16, kind="Internal").ap()
    cc_in = nc.dram_tensor("cc1_in", [128, 2048], BF16, kind="Internal").ap()
    cc_out = nc.dram_tensor("cc1_out", [256, 2048], BF16, kind="Internal").ap()
    w = declare_attn1_weights(nc)
    with ExitStack() as stack:
        es = stack.enter_context
        x = es(nc.sbuf_tensor("x", [128, NCH * T], F32))
        xn = es(nc.sbuf_tensor("xn", [128, NCH * T], BF16))
        b = alloc_consts(nc, es)
        b.ident_bf = es(nc.sbuf_tensor("ident_bf", [128, 128], BF16))
        ctxs = make_ctxs(nc, es)
        P = Prog(nc, ctxs)
        hw = ctxs["hw"]
        setup_consts(P, b)
        tok_x = P.run("sync", lambda e: e.dma_start(out=x[:], in_=xin[:, :]), inc=hw)
        P.run("sync", lambda e: e.dma_start(out=b.ident_bf[:], in_=ident[:, :]), inc=hw)
        pspool = [es(nc.psum_tensor(f"pspool{i}", [128, T], F32)) for i in range(3)]
        st = alloc_ln_sbuf(nc, es)
        ln_psum_views(st, pspool[1], pspool[2])
        xn_toks, _ = emit_ln(P, b, st, x[:], xn[:], tok_x)
        tok = emit_attn1(P, b, es, pspool, x[:], xn[:], xn_toks, st, w, qsc, cc_in, cc_out,
                         groups=groups)
        alltoks = [tok, (P.ctxs["dve"], P.ctxs["dve"].n), (P.ctxs["act"], P.ctxs["act"].n),
                   (P.ctxs["pe"], P.ctxs["pe"].n)]
        tst = P.run("sync", lambda e: e.dma_start(out=xout[:, :], in_=x[:]),
                    waits=alltoks, inc=hw)
        P.run("sync", lambda e: e.nop(), waits=[tst])
        P.emit()
    return nc



def ffn0_weight_arrays(inputs):
    gateT = np.ascontiguousarray(inputs["L0_ffn_gate_w"].T)
    upT = np.ascontiguousarray(inputs["L0_ffn_up_w"].T)
    downT = np.ascontiguousarray(inputs["L0_ffn_down_w"].T)
    sg = np.ascontiguousarray(inputs["L0_ffn_gate_scale"].reshape(64, 128).T)
    su = np.ascontiguousarray(inputs["L0_ffn_up_scale"].reshape(64, 128).T)
    sd = np.ascontiguousarray(inputs["L0_ffn_down_scale"].reshape(16, 128).T)
    return {"gateT": gateT, "upT": upT, "downT": downT, "sg": sg, "su": su, "sd": sd}


def declare_ffn0_weights(nc):
    w = {}
    w["gateT"] = nc.dram_tensor("w_gateT", [D, I0], I32, kind="ExternalInput").ap()
    w["upT"] = nc.dram_tensor("w_upT", [D, I0], I32, kind="ExternalInput").ap()
    w["downT"] = nc.dram_tensor("w_downT", [I0, D], I32, kind="ExternalInput").ap()
    w["sg"] = nc.dram_tensor("w_sg", [128, 64], F32, kind="ExternalInput").ap()
    w["su"] = nc.dram_tensor("w_su", [128, 64], F32, kind="ExternalInput").ap()
    w["sd"] = nc.dram_tensor("w_sd", [128, 16], F32, kind="ExternalInput").ap()
    return w


def alloc_consts(nc, es):
    b = Bufs()
    b.ones_col_bf = es(nc.sbuf_tensor("ones_col_bf", [128, 1], BF16))
    b.ones_col_f32 = es(nc.sbuf_tensor("ones_col_f32", [128, 1], F32))
    b.ones_row_bf = es(nc.sbuf_tensor("ones_row_bf", [1, 128], BF16))
    b.ones_row_f32 = es(nc.sbuf_tensor("ones_row_f32", [1, 128], F32))
    b.neg128_row = es(nc.sbuf_tensor("neg128_row", [1, 128], F32))
    b.c128_row = es(nc.sbuf_tensor("c128_row", [1, 128], F32))
    b.one_one = es(nc.sbuf_tensor("one_one", [1, 1], F32))
    b.eps = es(nc.sbuf_tensor("c_eps", [1, 1], F32))
    return b


def make_ctxs(nc, es):
    ctxs = {}
    for name, step, dma in (("pe", 1, False), ("act", 1, False), ("dve", 1, False),
                            ("pool", 1, False), ("hw", 16, True), ("sw", 16, True),
                            ("cc", 1, True)):
        sem = es(nc.semaphore("s_" + name))
        ctxs[name] = EngCtx(name, sem, step, dma)
    return ctxs


def build_ffn0_launch():
    from contextlib import ExitStack
    nc = bass.Bass("TRN2", target_bir_lowering=False, detect_race_conditions=False)
    xin = nc.dram_tensor("xin", [128, NCH * T], F32, kind="ExternalInput").ap()
    xout = nc.dram_tensor("xout", [128, NCH * T], F32, kind="ExternalOutput").ap()
    w = declare_ffn0_weights(nc)
    with ExitStack() as stack:
        es = stack.enter_context
        x = es(nc.sbuf_tensor("x", [128, NCH * T], F32))
        xn = es(nc.sbuf_tensor("xn", [128, NCH * T], BF16))
        b = alloc_consts(nc, es)
        ctxs = make_ctxs(nc, es)
        P = Prog(nc, ctxs)
        hw = ctxs["hw"]
        setup_consts(P, b)
        tok_x = P.run("sync", lambda e: e.dma_start(out=x[:], in_=xin[:, :]), inc=hw)
        pspool = [es(nc.psum_tensor(f"pspool{i}", [128, T], F32)) for i in range(4)]
        st = alloc_ln_sbuf(nc, es)
        ln_psum_views(st, pspool[2], pspool[3])
        xn_toks, _ = emit_ln(P, b, st, x[:], xn[:], tok_x)
        tok = emit_ffn0(P, b, es, pspool, x[:], xn[:], xn_toks, w)
        tst = P.run("sync", lambda e: e.dma_start(out=xout[:, :], in_=x[:]),
                    waits=[tok, (P.ctxs["dve"], P.ctxs["dve"].n)], inc=hw)
        P.run("sync", lambda e: e.nop(), waits=[tst])
        P.emit()
    return nc


def attn0_weight_arrays(inputs):
    wq = np.ascontiguousarray(inputs["L0_attn_Wq"], dtype=np.float32)
    wk = np.ascontiguousarray(inputs["L0_attn_Wk"], dtype=np.float32)
    wv = np.ascontiguousarray(inputs["L0_attn_Wv"], dtype=np.float32)
    wo = np.ascontiguousarray(inputs["L0_attn_Wo"], dtype=np.float32)
    bo_cb = np.ascontiguousarray(np.asarray(inputs["L0_attn_bo"], np.float32).reshape(NCH, 128).T)
    return {"wq": wq, "wk": wk, "wv": wv, "wo": wo, "bo_cb": bo_cb}


def attn0_masks(parity):
    i = np.arange(128)[:, None]
    j = np.arange(128)[None, :]
    tri = np.where(j > i, np.float32(NEG), np.float32(0.0))
    mrow = np.zeros((8, 2048), np.float32)
    diag = np.zeros((8, 128, 256), np.float32)
    for qt in range(8):
        if parity == 0:
            for kc in range(qt + 1, qt + 9):
                mrow[qt, kc * 128:(kc + 1) * 128] = NEG
            diag[qt, :, 0:128] = tri
        else:
            diag[qt, :, 128:256] = tri
    return mrow.astype(ml_dtypes.bfloat16), diag.astype(ml_dtypes.bfloat16)


def declare_attn0_weights(nc):
    w = {}
    w["wq"] = nc.dram_tensor("w_wq", [D, D], F32, kind="ExternalInput").ap()
    w["wk"] = nc.dram_tensor("w_wk", [D, 128], F32, kind="ExternalInput").ap()
    w["wv"] = nc.dram_tensor("w_wv", [D, 128], F32, kind="ExternalInput").ap()
    w["wo"] = nc.dram_tensor("w_wo", [D, D], F32, kind="ExternalInput").ap()
    w["bo_cb"] = nc.dram_tensor("w_bo", [128, NCH], F32, kind="ExternalInput").ap()
    w["maskrow"] = nc.dram_tensor("w_mrow", [8, 2048], BF16, kind="ExternalInput").ap()
    w["diag"] = nc.dram_tensor("w_diag", [8, 128, 256], BF16, kind="ExternalInput").ap()
    return w


def build_attn0_launch(groups=GROUPS, debug=False, nqt=8, nh=16):
    from contextlib import ExitStack
    nc = bass.Bass("TRN2", target_bir_lowering=False, detect_race_conditions=False)
    xin = nc.dram_tensor("xin", [128, NCH * T], F32, kind="ExternalInput").ap()
    xout = nc.dram_tensor("xout", [128, NCH * T], F32, kind="ExternalOutput").ap()
    ident = nc.dram_tensor("w_ident", [128, 128], BF16, kind="ExternalInput").ap()
    cc_in = nc.dram_tensor("cc0_in", [256, T], BF16, kind="Internal").ap()
    cc_out = nc.dram_tensor("cc0_out", [512, T], BF16, kind="Internal").ap()
    w = declare_attn0_weights(nc)
    with ExitStack() as stack:
        es = stack.enter_context
        x = es(nc.sbuf_tensor("x", [128, NCH * T], F32))
        xn = es(nc.sbuf_tensor("xn", [128, NCH * T], BF16))
        b = alloc_consts(nc, es)
        b.ident_bf = es(nc.sbuf_tensor("ident_bf", [128, 128], BF16))
        ctxs = make_ctxs(nc, es)
        P = Prog(nc, ctxs)
        hw = ctxs["hw"]
        setup_consts(P, b)
        tok_x = P.run("sync", lambda e: e.dma_start(out=x[:], in_=xin[:, :]), inc=hw)
        tok_id = P.run("sync", lambda e: e.dma_start(out=b.ident_bf[:], in_=ident[:, :]), inc=hw)
        pspool = [es(nc.psum_tensor(f"pspool{i}", [128, T], F32)) for i in range(3)]
        st = alloc_ln_sbuf(nc, es)
        ln_psum_views(st, pspool[1], pspool[2])
        xn_toks, _ = emit_ln(P, b, st, x[:], xn[:], tok_x)
        P.watermark
        tok = emit_attn0(P, b, es, pspool, x[:], xn[:], xn_toks, w, cc_in, cc_out, groups=groups, nqt=nqt, nh=nh)
        alltoks = [tok, (P.ctxs["dve"], P.ctxs["dve"].n), (P.ctxs["act"], P.ctxs["act"].n),
                   (P.ctxs["pe"], P.ctxs["pe"].n)]
        tst = P.run("sync", lambda e: e.dma_start(out=xout[:, :], in_=x[:]),
                    waits=alltoks, inc=hw)
        if debug:
            for nm, tens in P.dbg.items():
                shape = [128, tens.shape[1]]
                dt = nc.dram_tensor("dbg_" + nm, shape, BF16, kind="ExternalOutput").ap()
                tst = P.run("sync", lambda e, dt=dt, tens=tens: e.dma_start(out=dt[:, :], in_=tens[:]),
                            inc=hw)
            dsr = nc.dram_tensor("dbg_srows", [128, 8], F32, kind="ExternalOutput").ap()
            tst = P.run("sync", lambda e, dsr=dsr: e.dma_start(out=dsr[:, :], in_=P.dbg_srows[:]),
                        inc=hw)
        P.run("sync", lambda e: e.nop(), waits=[tst, tok_id])
        P.emit()
    return nc



def build_fused(groups=GROUPS):
    from contextlib import ExitStack
    nc = bass.Bass("TRN2", target_bir_lowering=False, detect_race_conditions=False)
    xin = nc.dram_tensor("xin", [128, NCH * T], F32, kind="ExternalInput").ap()
    xout = nc.dram_tensor("xout", [128, NCH * T], F32, kind="ExternalOutput").ap()
    ident = nc.dram_tensor("w_ident", [128, 128], BF16, kind="ExternalInput").ap()
    qsc = nc.dram_tensor("q_scratch", [128, NCH * T], BF16, kind="Internal").ap()
    cc0_in = nc.dram_tensor("cc0_in", [256, T], BF16, kind="Internal").ap()
    cc0_out = nc.dram_tensor("cc0_out", [512, T], BF16, kind="Internal").ap()
    cc1_in = nc.dram_tensor("cc1_in", [128, 2048], BF16, kind="Internal").ap()
    cc1_out = nc.dram_tensor("cc1_out", [256, 2048], BF16, kind="Internal").ap()
    w_f0 = declare_ffn0_weights(nc)
    w_a0 = declare_attn0_weights(nc)
    w_m = declare_moe_weights(nc)
    w_a1 = declare_attn1_weights(nc)
    with ExitStack() as stack:
        es = stack.enter_context
        x = es(nc.sbuf_tensor("x", [128, NCH * T], F32))
        xn = es(nc.sbuf_tensor("xn", [128, NCH * T], BF16))
        b = alloc_consts(nc, es)
        b.ident_bf = es(nc.sbuf_tensor("ident_bf", [128, 128], BF16))
        ctxs = make_ctxs(nc, es)
        P = Prog(nc, ctxs)
        hw = ctxs["hw"]
        st = alloc_ln_sbuf(nc, es)
        setup_consts(P, b)
        tok_x = P.run("sync", lambda e: e.dma_start(out=x[:], in_=xin[:, :]), inc=hw)
        P.run("sync", lambda e: e.dma_start(out=b.ident_bf[:], in_=ident[:, :]), inc=hw)

        def phase(kind):
            nonlocal tok_x
            with ExitStack() as pstk:
                pes = pstk.enter_context
                if kind in ("f0", "m"):
                    pspool = [pes(nc.psum_tensor(uname("p"), [128, T], F32)) for i in range(4)]
                    ln_psum_views(st, pspool[2], pspool[3])
                else:
                    pspool = [pes(nc.psum_tensor(uname("p"), [128, T], F32)) for i in range(3)]
                    ln_psum_views(st, pspool[1], pspool[2])
                xn_toks, _ = emit_ln(P, b, st, x[:], xn[:], tok_x)
                if kind == "f0":
                    tok = emit_ffn0(P, b, pes, pspool, x[:], xn[:], xn_toks, w_f0)
                elif kind == "a0":
                    tok = emit_attn0(P, b, pes, pspool, x[:], xn[:], xn_toks, w_a0,
                                     cc0_in, cc0_out, groups=groups)
                elif kind == "m":
                    tok = emit_moe(P, b, pes, pspool, x[:], xn[:], xn_toks, st, w_m)
                elif kind == "a1":
                    tok = emit_attn1(P, b, pes, pspool, x[:], xn[:], xn_toks, st, w_a1,
                                     qsc, cc1_in, cc1_out, groups=groups)
                P.emit()
            tok_x = tok
            return tok

        for kind in ("f0", "a0", "f0", "m", "a1", "m"):
            phase(kind)

        alltoks = [tok_x, (P.ctxs["dve"], P.ctxs["dve"].n), (P.ctxs["act"], P.ctxs["act"].n),
                   (P.ctxs["pe"], P.ctxs["pe"].n)]
        tst = P.run("sync", lambda e: e.dma_start(out=xout[:, :], in_=x[:]),
                    waits=alltoks, inc=hw)
        P.run("sync", lambda e: e.nop(), waits=[tst])
        P.emit()
    return nc


def fused_in_map(inputs, shards, c):
    import ml_dtypes
    ident = np.eye(128, dtype=ml_dtypes.bfloat16)
    w_f0 = inputs["_w_f0"]
    w_a0 = inputs["_w_a0"]
    w_m = inputs["_w_m"]
    w_a1 = inputs["_w_a1"]
    mrow, diag = inputs["_masks"][c % 2]
    return {"xin": shards[c], "w_ident": ident,
            "w_gateT": w_f0["gateT"], "w_upT": w_f0["upT"], "w_downT": w_f0["downT"],
            "w_sg": w_f0["sg"], "w_su": w_f0["su"], "w_sd": w_f0["sd"],
            "w_wq": w_a0["wq"], "w_wk": w_a0["wk"], "w_wv": w_a0["wv"],
            "w_wo": w_a0["wo"], "w_bo": w_a0["bo_cb"], "w_mrow": mrow, "w_diag": diag,
            "w_rcb": w_m["rcb"], "w_m_gateT": w_m["gateT"], "w_m_upT": w_m["upT"],
            "w_m_downT": w_m["downT"],
            "w_wqk": w_a1["wqk"], "w_wv1": w_a1["wv"], "w_wu1": w_a1["wu"],
            "w_wo1": w_a1["wo"], "w_bias6": w_a1["bias6"]}



def shard_x(x_full):
    shards = []
    for c in range(NCORE):
        bb, hh = c // 2, c % 2
        xs = x_full[bb, hh * T:(hh + 1) * T, :]
        xs = np.ascontiguousarray(xs.T)
        xs = xs.reshape(NCH, 128, T).transpose(1, 0, 2).reshape(128, NCH * T)
        shards.append(np.ascontiguousarray(xs))
    return shards


def unshard_x(shards):
    out = np.empty((4, 2048, D), np.float32)
    for c in range(NCORE):
        bb, hh = c // 2, c % 2
        xs = shards[c].reshape(128, NCH, T).transpose(1, 0, 2).reshape(D, T)
        out[bb, hh * T:(hh + 1) * T, :] = xs.T
    return out


def kernel(**inputs):
    x = np.asarray(inputs["L0_attn_x"], np.float32)
    shards = shard_x(x)
    prep = {"_w_f0": ffn0_weight_arrays(inputs), "_w_a0": attn0_weight_arrays(inputs),
            "_w_m": moe_weight_arrays(inputs), "_w_a1": attn1_weight_arrays(inputs),
            "_masks": [attn0_masks(0), attn0_masks(1)]}
    nc = build_fused()
    in_maps = [fused_in_map(prep, shards, c) for c in range(NCORE)]
    res = run_bass_kernel_spmd(nc, in_maps, core_ids=list(range(NCORE)))
    return unshard_x([np.asarray(res.results[c]["xout"]) for c in range(NCORE)])


# version 08
# speedup vs baseline: 1.1406x; 1.1406x over previous
import numpy as np
import ml_dtypes
import concourse.bass as bass
import concourse.mybir as mybir
from concourse.bass_utils import run_bass_kernel_spmd

F32 = mybir.dt.float32
BF16 = mybir.dt.bfloat16
I32 = mybir.dt.int32
AF = mybir.ActivationFunctionType
ALU = mybir.AluOpType
AX = mybir.AxisListType

NCORE = 8
T = 1024
D = 2048
NCH = D // 128
I0 = 8192
I1 = 1408
NE = 8
GROUPS = [[0, 1], [2, 3], [4, 5], [6, 7]]
EPS = 1e-5
NEG = -1e30



class EngCtx:
    def __init__(self, name, sem, step=1, is_dma=False):
        self.name = name
        self.sem = sem
        self.step = step
        self.is_dma = is_dma
        self.n = 0


class Prog:

    ENGS = ("sync", "gpsimd", "vector", "scalar", "tensor")

    def __init__(self, nc, ctxs):
        self.nc = nc
        self.ctxs = ctxs
        self.thunks = {k: [] for k in self.ENGS}
        self.watermark = {k: {} for k in self.ENGS}

    def run(self, eng, fn, waits=(), inc=None, inc_k=None):
        th = self.thunks[eng]
        wm = self.watermark[eng]
        for w in waits:
            if w is None:
                continue
            src, val = w
            if wm.get(src.name, 0) >= val:
                continue
            wm[src.name] = val
            th.append(lambda e, s=src.sem, v=val: e.wait_ge(s, v))
        tok = None
        if inc is not None:
            k = inc_k if inc_k is not None else inc.step
            inc.n += k
            tok = (inc, inc.n)
            th.append(lambda e, f=fn, s=inc.sem, kk=k: f(e).then_inc(s, kk))
        else:
            th.append(lambda e, f=fn: f(e))
        return tok

    def barrier_wait(self, eng, toks):
        self.run(eng, lambda e: e.engine_nop() if eng in ("vector", "gpsimd") else e.nop(),
                 waits=toks)

    def emit(self):
        blk_mgr = self.nc.Block(no_gpsimd_drain=True)
        with blk_mgr as block:
            for name in self.ENGS:
                ths = self.thunks[name]
                if not ths:
                    continue
                section = getattr(block, name)

                def body(e, ths=ths):
                    for t in ths:
                        t(e)
                section(body)
        self.thunks = {k: [] for k in self.ENGS}
        self.watermark = {k: {} for k in self.ENGS}



def ceil_div(a, b):
    return (a + b - 1) // b


_UID = [0]


def uname(tag):
    _UID[0] += 1
    return f"{tag}{_UID[0]}"


class Bufs(dict):
    __getattr__ = dict.__getitem__
    __setattr__ = dict.__setitem__


def setup_consts(P, b):
    dve = P.ctxs["dve"]
    P.run("vector", lambda e: e.memset(b.ones_col_bf[:], 1.0))
    P.run("vector", lambda e: e.memset(b.ones_col_f32[:], 1.0))
    P.run("vector", lambda e: e.memset(b.ones_row_bf[:], 1.0))
    P.run("vector", lambda e: e.memset(b.ones_row_f32[:], 1.0))
    P.run("vector", lambda e: e.memset(b.neg128_row[:], -128.0))
    P.run("vector", lambda e: e.memset(b.c128_row[:], 128.0))
    P.run("vector", lambda e: e.memset(b.neg128_t[:], 0.0))
    P.run("vector", lambda e: e.memset(b.neg128_t[0:1, :], -128.0))
    P.run("vector", lambda e: e.memset(b.ones_t[:], 0.0))
    P.run("vector", lambda e: e.memset(b.ones_t[0:1, :], 1.0))
    P.run("vector", lambda e: e.memset(b.one_one[:], 1.0))
    t = P.run("vector", lambda e: e.memset(b.eps[:], EPS), inc=dve)
    return t



def emit_ln(P, b, st, x, xn, tok_x):
    pe, act, dve = P.ctxs["pe"], P.ctxs["act"], P.ctxs["dve"]
    cp_toks = []
    for c in range(NCH):
        sl = slice(c * T, (c + 1) * T)
        t1 = P.run("vector", lambda e, sl=sl, c=c: e.tensor_copy(
            st.cbuf[:, (c % 2) * T:(c % 2) * T + T], x[:, sl]),
            waits=[tok_x, st.cbuf_free[c % 2]], inc=dve)
        t2 = P.run("scalar", lambda e, sl=sl, c=c: e.activation(
            st.qbuf[:, (c % 2) * T:(c % 2) * T + T], x[:, sl], AF.Square),
            waits=[tok_x, st.qbuf_free[c % 2]], inc=act)
        for hh in range(2):
            hsl = slice((c % 2) * T + hh * 512, (c % 2) * T + hh * 512 + 512)
            osl = slice(hh * 512, hh * 512 + 512)
            P.run("tensor", lambda e, hsl=hsl, osl=osl, c=c: e.matmul(
                st.ps_s1[:, osl], b.ones_col_bf[:], st.cbuf[:, hsl],
                start=(c == 0), stop=(c == NCH - 1)),
                waits=[t1])
            tok_mm = P.run("tensor", lambda e, hsl=hsl, osl=osl, c=c: e.matmul(
                st.ps_s2[:, osl], b.ones_col_bf[:], st.qbuf[:, hsl],
                start=(c == 0), stop=(c == NCH - 1)),
                waits=[t2], inc=pe)
        st.cbuf_free[c % 2] = tok_mm
        st.qbuf_free[c % 2] = tok_mm
    tm = P.run("vector", lambda e: e.tensor_scalar_mul(st.row_m[:], st.ps_s1[:, :], 1.0 / D),
               waits=[tok_mm], inc=dve)
    te = P.run("vector", lambda e: e.tensor_scalar_mul(st.row_a[:], st.ps_s2[:, :], 1.0 / D),
               waits=[tok_mm], inc=dve)
    tq = P.run("scalar", lambda e: e.activation(st.row_b[:], st.row_m[:], AF.Square),
               waits=[tm], inc=act)
    tv = P.run("vector", lambda e: e.tensor_sub(st.row_a[:], st.row_a[:], st.row_b[:]),
               waits=[te, tq], inc=dve)
    ts = P.run("scalar", lambda e: e.activation(st.row_b[:], st.row_a[:], AF.Sqrt, bias=b.eps[0:1, 0:1]),
               waits=[tv], inc=act)
    tr = P.run("vector", lambda e: e.reciprocal(st.row_r[:], st.row_b[:]),
               waits=[ts], inc=dve)
    st.tok_m = tm
    st.tok_r = tr
    tmr = P.run("vector", lambda e: e.tensor_mul(st.row_mr[:], st.row_m[:], st.row_r[:]),
                waits=[tr], inc=dve)
    br_toks = []
    for hh in range(2):
        osl = slice(hh * 512, hh * 512 + 512)
        P.run("tensor", lambda e, osl=osl: e.matmul(
            st.ps_br[:][:, osl], b.ones_row_f32[:], st.row_r[0:1, osl],
            start=True, stop=True), waits=[tr])
        tb = P.run("tensor", lambda e, osl=osl: e.matmul(
            st.ps_bmr[:][:, osl], b.ones_row_f32[:], st.row_mr[0:1, osl],
            start=True, stop=True), waits=[tmr], inc=pe)
    xn_toks = []
    for c in range(NCH):
        sl = slice(c * T, (c + 1) * T)
        P.run("vector", lambda e, sl=sl, c=c: e.tensor_mul(
            st.cbuf[:, (c % 2) * T:(c % 2) * T + T], x[:, sl], st.ps_br[:][:, :]),
            waits=[tb, st.cbuf_free[c % 2]])
        tc2 = P.run("vector", lambda e, sl=sl, c=c: e.tensor_sub(
            xn[:, sl], st.cbuf[:, (c % 2) * T:(c % 2) * T + T], st.ps_bmr[:][:, :]),
            inc=dve)
        st.cbuf_free[c % 2] = tc2
        xn_toks.append(tc2)
    return xn_toks, tc2


def alloc_ln_sbuf(nc, es):
    st = Bufs()
    st.cbuf = es(nc.sbuf_tensor("ln_cbuf", [128, 2 * T], BF16))
    st.qbuf = es(nc.sbuf_tensor("ln_qbuf", [128, 2 * T], BF16))
    st.row_m = es(nc.sbuf_tensor("row_m", [1, T], F32))
    st.row_a = es(nc.sbuf_tensor("row_a", [1, T], F32))
    st.row_b = es(nc.sbuf_tensor("row_b", [1, T], F32))
    st.row_r = es(nc.sbuf_tensor("row_r", [1, T], F32))
    st.row_mr = st.row_a
    return st


def ln_psum_views(st, pa, pb):
    st.cbuf_free = [None, None]
    st.qbuf_free = [None, None]
    st.ps_s1 = pa[0:1, :]
    st.ps_s2 = pb[0:1, :]
    st.ps_br = pa
    st.ps_bmr = pb
    return st



def emit_ffn0(P, b, es, pspool, x, xn, xn_toks, w, half_scale=0.5):
    nc = P.nc
    pe, act, dve, hw, sw = (P.ctxs[k] for k in ("pe", "act", "dve", "hw", "sw"))
    xn_done = xn_toks[-1]

    NB = 16
    NBATCH = 2
    NGRP = I0 // (NB * 128)
    GW = NBATCH * 128
    DNB = 8

    wg = es(nc.sbuf_tensor(uname("s"), [128, 2 * NCH * GW], BF16))
    wd = es(nc.sbuf_tensor(uname("s"), [128, 2 * DNB * 512], BF16))
    sgb = es(nc.sbuf_tensor(uname("s"), [128, NBATCH * T], BF16))
    actb = es(nc.sbuf_tensor(uname("s"), [128, NB * T], BF16))
    ps = pspool
    ps_row = ps[0][0:1, :]
    rows = es(nc.sbuf_tensor(uname("s"), [1, 2 * T], F32))
    rows_t = es(nc.sbuf_tensor(uname("s"), [128, 2 * T], BF16))
    scl = es(nc.sbuf_tensor(uname("s"), [128, 64 + 64 + 16 + 16], F32))

    P.run("sync", lambda e: e.dma_start(out=scl[:, 0:64], in_=w["sg"][:, :]), inc=hw)
    P.run("sync", lambda e: e.dma_start(out=scl[:, 64:128], in_=w["su"][:, :]), inc=hw)
    P.run("sync", lambda e: e.dma_start(out=scl[:, 128:144], in_=w["sd"][:, :]), inc=hw)
    tok_scl = (hw, hw.n)
    tsd2 = P.run("vector", lambda e: e.tensor_scalar_mul(
        scl[:, 144:160], scl[:, 128:144], half_scale), waits=[tok_scl], inc=dve)

    for c in range(NCH):
        for hh in range(2):
            tok_sx = P.run("tensor", lambda e, c=c, hh=hh: e.matmul(
                ps_row[:, hh * 512:hh * 512 + 512], b.ones_col_bf[:],
                xn[:, c * T + hh * 512: c * T + hh * 512 + 512],
                start=(c == 0), stop=(c == NCH - 1)),
                waits=[xn_toks[c]], inc=(pe if (c == NCH - 1 and hh == 1) else None))
    tok_rt0 = P.run("vector", lambda e: e.memset(rows_t[:], 0.0), inc=dve)
    tok_sxe = P.run("vector", lambda e: e.tensor_copy(rows_t[0:1, 0:T], ps_row[:, :]),
                    waits=[tok_sx, tok_rt0], inc=dve)

    def dma_gu(mat, g, bb, slot, wait):
        i0 = (g * NB + bb * NBATCH) * 128
        src = mat.rearrange("(c p) i -> p c i", p=128)[:, :, i0:i0 + GW]
        return P.run("gpsimd", lambda e, src=src, slot=slot: e.dma_start(
            out=wg[:, slot * NCH * GW:(slot + 1) * NCH * GW].rearrange(
                "p (c j) -> p c j", c=NCH),
            in_=src), waits=[wait], inc=sw)

    def dma_dn(g, hq, s2, slot, wait):
        c0 = g * NB + s2 * DNB
        src = w["downT"].rearrange("(c p) h -> p c h", p=128)[
            :, c0:c0 + DNB, hq * 512:(hq + 1) * 512]
        return P.run("gpsimd", lambda e, src=src, slot=slot: e.dma_start(
            out=wd[:, slot * DNB * 512:(slot + 1) * DNB * 512].rearrange(
                "p (c j) -> p c j", c=DNB),
            in_=src), waits=[wait], inc=sw)

    NBB = NB // NBATCH
    gu_seq = [(mt, g, bb) for g in range(NGRP) for bb in range(NBB) for mt in ("gu"[0], "gu"[1])]
    gu_seq = []
    for g in range(NGRP):
        for bb in range(NBB):
            gu_seq.append(("g", g, bb))
            gu_seq.append(("u", g, bb))
    gu_toks, gu_slot = {}, {}
    slot_free = [None, None]
    for idx in range(2):
        mt, g, bb = gu_seq[idx]
        gu_toks[(mt, g, bb)] = dma_gu(w["gateT"] if mt == "g" else w["upT"], g, bb,
                                      idx % 2, slot_free[idx % 2])
        gu_slot[(mt, g, bb)] = idx % 2
    gu_next = 2

    dn_seq = [(g, hq, s2) for g in range(NGRP) for hq in range(4) for s2 in range(2)]
    dn_toks, dn_slot = {}, {}
    dn_slot_free = [None, None]
    dn_slot_free[1] = None
    t_wdinit = P.run("vector", lambda e: e.memset(wd[:], 0.0), inc=dve)
    dn_toks[dn_seq[0]] = dma_dn(0, 0, 0, 0, t_wdinit)
    dn_slot[dn_seq[0]] = 0
    dn_next = 1

    ps_free = [tok_sxe, None, None, None]
    actb_free = [None] * NB
    sgb_free = [None, None]
    tok_act = None

    for g in range(NGRP):
        for bb in range(NBB):
            for mt in ("g", "u"):
                key = (mt, g, bb)
                tok_w = gu_toks.pop(key)
                slot = gu_slot.pop(key)
                base = slot * NCH * GW
                for hc in range(NCH):
                    for ib in range(NBATCH):
                        for hh in range(2):
                            wl = [tok_w, xn_done]
                            if hc == 0 and hh == 0:
                                wl.append(ps_free[ib])
                            P.run("tensor", lambda e, hc=hc, ib=ib, hh=hh, base=base: e.matmul(
                                ps[ib][:, hh * 512:hh * 512 + 512],
                                wg[:, base + hc * GW + ib * 128: base + hc * GW + ib * 128 + 128],
                                xn[:, hc * T + hh * 512: hc * T + hh * 512 + 512],
                                start=(hc == 0), stop=False),
                                waits=wl)
                for ib in range(NBATCH):
                    for hh in range(2):
                        tok_fix = P.run("tensor", lambda e, ib=ib, hh=hh: e.matmul(
                            ps[ib][:, hh * 512:hh * 512 + 512],
                            b.neg128_t[:],
                            rows_t[:, hh * 512:hh * 512 + 512],
                            start=False, stop=True),
                            waits=[tok_sxe],
                            inc=(pe if (ib == NBATCH - 1 and hh == 1) else None))
                slot_free[slot] = tok_fix
                if gu_next < len(gu_seq):
                    mt2, g2, bb2 = gu_seq[gu_next]
                    s2 = gu_next % 2
                    gu_toks[(mt2, g2, bb2)] = dma_gu(
                        w["gateT"] if mt2 == "g" else w["upT"], g2, bb2, s2, slot_free[s2])
                    gu_slot[(mt2, g2, bb2)] = s2
                    gu_next += 1
                ibase = bb * NBATCH
                if mt == "g":
                    for ib in range(NBATCH):
                        col = g * NB + ibase + ib
                        tok_sg = P.run("scalar", lambda e, ib=ib, col=col: e.activation(
                            sgb[:, ib * T:(ib + 1) * T], ps[ib][:, :], AF.Silu,
                            scale=scl[:, col:col + 1]),
                            waits=[tok_fix, tok_scl, sgb_free[ib]], inc=act)
                        ps_free[ib] = tok_sg
                else:
                    for ib in range(NBATCH):
                        col = 64 + g * NB + ibase + ib
                        tok_act = P.run("vector", lambda e, ib=ib, col=col, ibase=ibase: e.scalar_tensor_tensor(
                            actb[:, (ibase + ib) * T:(ibase + ib + 1) * T],
                            ps[ib][:, :], scl[:, col:col + 1], sgb[:, ib * T:(ib + 1) * T],
                            ALU.mult, ALU.mult),
                            waits=[tok_fix, tok_sg, actb_free[ibase + ib]], inc=dve)
                        ps_free[ib] = tok_act
                        sgb_free[ib] = tok_act
        for ib in range(NB):
            for hh in range(2):
                wl_sa = [tok_act]
                if ib == 0 and hh == 0:
                    wl_sa.append(ps_free[0])
                tok_sa = P.run("tensor", lambda e, ib=ib, hh=hh: e.matmul(
                    ps_row[:, hh * 512:hh * 512 + 512], b.ones_col_bf[:],
                    actb[:, ib * T + hh * 512: ib * T + hh * 512 + 512],
                    start=(ib == 0), stop=(ib == NB - 1)),
                    waits=wl_sa, inc=(pe if (ib == NB - 1 and hh == 1) else None))
        tok_sae = P.run("vector", lambda e: e.tensor_copy(rows_t[0:1, T:2 * T], ps_row[:, :]),
                        waits=[tok_sa], inc=dve)
        ps_free[0] = tok_sae
        for hq in range(4):
            for s2 in range(2):
                key = (g, hq, s2)
                tok_dw = dn_toks.pop(key)
                dslot = dn_slot.pop(key)
                dbase = dslot * DNB * 512
                for icl in range(DNB):
                    ic = s2 * DNB + icl
                    for hcq in range(4):
                        for hh in range(2):
                            wl = [tok_dw]
                            if ic == 0 and hh == 0:
                                wl.append(ps_free[hcq])
                                wl.append(tok_sae)
                            if ic == 0:
                                wl.append(tok_act)
                            is_last = (icl == DNB - 1 and hcq == 3 and hh == 1)
                            tok_dmm = P.run("tensor", lambda e, ic=ic, icl=icl, hcq=hcq, hh=hh, dbase=dbase, s2=s2: e.matmul(
                                ps[hcq][:, hh * 512:hh * 512 + 512],
                                wd[:, dbase + icl * 512 + hcq * 128: dbase + icl * 512 + hcq * 128 + 128],
                                actb[:, ic * T + hh * 512: ic * T + hh * 512 + 512],
                                start=(ic == 0), stop=False),
                                waits=wl, inc=(pe if is_last else None))
                dn_slot_free[dslot] = tok_dmm
                if dn_next < len(dn_seq):
                    g2, hq2, s22 = dn_seq[dn_next]
                    sl2 = dn_next % 2
                    dn_toks[(g2, hq2, s22)] = dma_dn(g2, hq2, s22, sl2, dn_slot_free[sl2])
                    dn_slot[(g2, hq2, s22)] = sl2
                    dn_next += 1
            for hcq in range(4):
                for hh in range(2):
                    tok_dfix = P.run("tensor", lambda e, hcq=hcq, hh=hh: e.matmul(
                        ps[hcq][:, hh * 512:hh * 512 + 512],
                        b.neg128_t[:],
                        rows_t[:, T + hh * 512: T + hh * 512 + 512],
                        start=False, stop=True),
                        waits=[tok_sae], inc=(pe if (hcq == 3 and hh == 1) else None))
            for hcq in range(4):
                hc = hq * 4 + hcq
                tok_ev = P.run("vector", lambda e, hcq=hcq, hc=hc: e.scalar_tensor_tensor(
                    x[:, hc * T:(hc + 1) * T], ps[hcq][:, :],
                    scl[:, 144 + hc:144 + hc + 1], x[:, hc * T:(hc + 1) * T],
                    ALU.mult, ALU.add),
                    waits=[tok_dfix, tsd2], inc=dve)
                ps_free[hcq] = tok_ev
        for ib in range(NB):
            actb_free[ib] = tok_dfix
    return tok_ev



def emit_proj(P, b, wst, wst_free, ppair, pp_free, xn, tok_ready, wap, out_w,
              evac, extra_mm_waits=()):
    sw, pe = P.ctxs["sw"], P.ctxs["pe"]
    nob = out_w // 128
    toks = {}

    def dma_w(ob, slot, wait):
        src = wap.rearrange("(c p) o -> p c o", p=128)[:, :, ob * 128:(ob + 1) * 128]
        return P.run("gpsimd", lambda e, src=src, slot=slot: e.dma_start(
            out=wst[:, slot * 2048:(slot + 1) * 2048].rearrange("p (c j) -> p c j", c=NCH),
            in_=src), waits=[wait], inc=sw)

    for idx in range(min(2, nob)):
        toks[idx] = dma_w(idx, idx % 2, wst_free[idx % 2])
    ev_toks = []
    for ob in range(nob):
        slot = ob % 2
        pt = ppair[ob % 2]
        tw = toks.pop(ob)
        base = slot * 2048
        for hc in range(NCH):
            for hh in range(2):
                wl = [tw, tok_ready] + list(extra_mm_waits)
                if hc == 0 and hh == 0:
                    wl.append(pp_free[ob % 2])
                tok_mm = P.run("tensor", lambda e, hc=hc, hh=hh, base=base, pt=pt: e.matmul(
                    pt[:, hh * 512:hh * 512 + 512],
                    wst[:, base + hc * 128: base + hc * 128 + 128],
                    xn[:, hc * T + hh * 512: hc * T + hh * 512 + 512],
                    start=(hc == 0), stop=(hc == NCH - 1)),
                    waits=wl, inc=(pe if (hc == NCH - 1 and hh == 1) else None))
        wst_free[slot] = tok_mm
        if ob + 2 < nob:
            toks[ob + 2] = dma_w(ob + 2, slot, wst_free[slot])
        tev = evac(ob, pt, tok_mm)
        pp_free[ob % 2] = tev
        ev_toks.append(tev)
    return ev_toks



SC0 = 1.0 / float(np.sqrt(128.0))


def emit_attn0(P, b, es, pspool, x, xn, xn_toks, w, cc_in, cc_out, groups=GROUPS, nqt=8, nh=16):
    nc = P.nc
    pe, act, dve, hw, sw, cc = (P.ctxs[k] for k in ("pe", "act", "dve", "hw", "sw", "cc"))
    xn_done = xn_toks[-1]
    p0, p1, p2 = pspool

    bt = [es(nc.psum_tensor(uname("p"), [128, 1024], BF16)) for i in range(2)]
    wst = es(nc.sbuf_tensor(uname("s"), [128, 2 * 2048], BF16))
    qf = es(nc.sbuf_tensor(uname("s"), [128, NCH * T], BF16))
    kfull = es(nc.sbuf_tensor(uname("s"), [128, 2048], BF16))
    vfull = es(nc.sbuf_tensor(uname("s"), [128, 2048], BF16))
    vtm = es(nc.sbuf_tensor(uname("s"), [128, 2048], BF16))
    pscr = es(nc.sbuf_tensor(uname("s"), [128, 4096], BF16))
    atT = es(nc.sbuf_tensor(uname("s"), [128, 2048], BF16))
    mrow = es(nc.sbuf_tensor(uname("s"), [128, 2048], BF16))
    diag = es(nc.sbuf_tensor(uname("s"), [128, 256], BF16))
    srows = es(nc.sbuf_tensor(uname("s"), [128, 8], F32))
    bocb = es(nc.sbuf_tensor(uname("s"), [128, NCH], F32))

    kself = pscr[:, 0:T]
    vself = pscr[:, T:2 * T]
    probs = [pscr[:, 0:2048], pscr[:, 2048:4096]]
    ao = xn
    if nqt < 8 or nh < 16:
        P.run("vector", lambda e: e.memset(pscr[:], 0.0))
        P.run("vector", lambda e: e.memset(atT[:], 0.0))
        P.run("vector", lambda e: e.memset(ao[:], 0.0))
        P.run("vector", lambda e: e.memset(srows[:], 0.0), inc=P.ctxs["dve"])

    tok_bo = P.run("sync", lambda e: e.dma_start(out=bocb[:], in_=w["bo_cb"][:, :]), inc=hw)

    wst_free = [None, None]
    pp_free = [None, None]

    def ev_q(ob, pt, tok_pe):
        return P.run("scalar", lambda e, ob=ob, pt=pt: e.activation(
            qf[:, ob * T:(ob + 1) * T], pt[:, :], AF.Copy, scale=SC0),
            waits=[tok_pe], inc=act)
    evq = emit_proj(P, b, wst, wst_free, [p0, p1], pp_free, xn, xn_done, w["wq"], D, ev_q)
    tok_qdone = evq[-1]

    def ev_k(ob, pt, tok_pe):
        return P.run("scalar", lambda e, pt=pt: e.activation(
            kself, pt[:, :], AF.Copy), waits=[tok_pe], inc=act)

    def ev_v(ob, pt, tok_pe):
        return P.run("scalar", lambda e, pt=pt: e.activation(
            vself, pt[:, :], AF.Copy), waits=[tok_pe], inc=act)
    tk = emit_proj(P, b, wst, wst_free, [p0, p1], pp_free, xn, xn_done, w["wk"], 128, ev_k)
    tv = emit_proj(P, b, wst, wst_free, [p0, p1], pp_free, xn, xn_done, w["wv"], 128, ev_v)

    t1 = P.run("sync", lambda e: e.dma_start(out=cc_in[0:128, :], in_=kself), waits=[tk[0]], inc=hw)
    t2 = P.run("sync", lambda e: e.dma_start(out=cc_in[128:256, :], in_=vself), waits=[tv[0]], inc=hw)
    tcc = P.run("gpsimd", lambda e, groups=groups: e.collective_compute(
        "AllGather", ALU.bypass, replica_groups=groups,
        ins=[cc_in.opt()], outs=[cc_out.opt()]),
        waits=[t1, t2], inc=cc, inc_k=1)
    P.run("sync", lambda e: e.dma_start(out=kfull[:, 0:T], in_=cc_out[0:128, :]), waits=[tcc], inc=hw)
    P.run("sync", lambda e: e.dma_start(out=kfull[:, T:2 * T], in_=cc_out[256:384, :]), inc=hw)
    P.run("sync", lambda e: e.dma_start(out=vfull[:, 0:T], in_=cc_out[128:256, :]), inc=hw)
    P.run("sync", lambda e: e.dma_start(out=vfull[:, T:2 * T], in_=cc_out[384:512, :]), inc=hw)
    tok_kf = (hw, hw.n)

    bt_free = [None, None]
    tok_vtm = None
    for half in range(2):
        bk = half
        for j in range(8):
            kc = half * 8 + j
            tok_tr = P.run("tensor", lambda e, j=j, kc=kc, bk=bk: e.transpose(
                bt[bk][:, j * 128:(j + 1) * 128], vfull[:, kc * 128:(kc + 1) * 128],
                b.ident_bf[:]), waits=[tok_kf, bt_free[bk]],
                inc=(pe if j == 7 else None))
        tok_vtm = P.run("vector", lambda e, half=half, bk=bk: e.tensor_copy(
            vtm[:, half * 1024:(half + 1) * 1024], bt[bk][:, :]),
            waits=[tok_tr], inc=dve)
        bt_free[bk] = tok_vtm

    mrow_free = diag_free = None
    scores_free = [pp_free[0], pp_free[1]]
    probs_free = [None, None]
    atT_free = None
    av_free = xn_done
    tok_ao = None
    tok_sc = None
    pending = None

    def emit_tail(qt, h, tok_pn):
        nonlocal atT_free, av_free, tok_ao, probs_free, bt_free
        nkc = 9 + qt
        slot = h % 2
        ev_toks = []
        tok_tr_last = None
        for half in range(ceil_div(nkc, 8)):
            bk = half % 2
            lo8 = half * 8
            n_in = min(8, nkc - lo8)
            for j in range(n_in):
                kc = lo8 + j
                tok_tr = P.run("tensor", lambda e, j=j, kc=kc, bk=bk, slot=slot: e.transpose(
                    bt[bk][:, j * 128:(j + 1) * 128], probs[slot][:, kc * 128:(kc + 1) * 128],
                    b.ident_bf[:]),
                    waits=[tok_pn, bt_free[bk]],
                    inc=(pe if j == n_in - 1 else None))
            tok_evc = P.run("vector", lambda e, bk=bk, lo8=lo8, n_in=n_in: e.tensor_copy(
                atT[:, lo8 * 128: lo8 * 128 + n_in * 128], bt[bk][:, 0:n_in * 128]),
                waits=[tok_tr, atT_free], inc=dve)
            bt_free[bk] = tok_evc
            ev_toks.append(tok_evc)
            tok_tr_last = tok_tr
        probs_free[slot] = tok_tr_last
        for kc in range(nkc):
            wl = [ev_toks[kc // 8], tok_vtm]
            if kc == 0:
                wl.append(av_free)
            tok_av = P.run("tensor", lambda e, kc=kc, nkc=nkc: e.matmul(
                p2[:, 0:128], vtm[:, kc * 128:(kc + 1) * 128], atT[:, kc * 128:(kc + 1) * 128],
                start=(kc == 0), stop=(kc == nkc - 1)),
                waits=wl, inc=(pe if kc == nkc - 1 else None))
        atT_free = tok_av
        tok_ao = P.run("scalar", lambda e, h=h, qt=qt: e.activation(
            ao[:, h * T + qt * 128: h * T + qt * 128 + 128], p2[:, 0:128], AF.Copy),
            waits=[tok_av], inc=act)
        av_free = tok_ao

    for qt in range(8):
        nkc = 9 + qt
        hi = (qt + 1) * 128
        if qt == 0:
            tmz = P.run("vector", lambda e: e.memset(mrow[:], 0.0), inc=dve)
        tok_mr = P.run("sync", lambda e, qt=qt: e.dma_start(
            out=mrow[0:1, :], in_=w["maskrow"][qt:qt + 1, :]),
            waits=[mrow_free, tmz], inc=hw)
        tok_dg = P.run("sync", lambda e, qt=qt: e.dma_start(
            out=diag[:, :], in_=w["diag"][qt, :, :]), waits=[diag_free], inc=hw)
        for h in range(NCH):
            slot = h % 2
            q0 = h * T + qt * 128
            if hi > 512:
                sg_a = [(p0, 0, 512, 0), (p0, 512, 512, 512)]
                sg_b = [(p1, 0, 512, 1024), (p1, 512, hi - 512, 1536)]
            else:
                sg_a = [(p0, 0, 512, 0), (p0, 512, 512, 512)]
                sg_b = [(p1, 0, hi, 1024)]

            def emit_sgroups(sgroups):
                nonlocal tok_sc
                for gi, (pt, off, wdt, key0) in enumerate(sgroups):
                    cands = []
                    if key0 <= qt * 128 < key0 + wdt:
                        cands.append((0, qt * 128 - key0))
                    if key0 <= 1024 + qt * 128 < key0 + wdt:
                        cands.append((1, 1024 + qt * 128 - key0))
                    wl = [scores_free[0 if pt is p0 else 1], tok_mr, tok_dg, tok_kf, tok_qdone]
                    P.run("tensor", lambda e, pt=pt, off=off, wdt=wdt, key0=key0, q0=q0: e.matmul(
                        pt[:, off:off + wdt], qf[:, q0:q0 + 128], kfull[:, key0:key0 + wdt],
                        start=True, stop=False), waits=wl)
                    for (cand, cpos) in cands:
                        P.run("tensor", lambda e, pt=pt, off=off, cpos=cpos, cand=cand: e.matmul(
                            pt[:, off + cpos:off + cpos + 128], b.ident_bf[:],
                            diag[:, cand * 128:(cand + 1) * 128],
                            start=False, stop=False))
                    tok_sc = P.run("tensor", lambda e, pt=pt, off=off, wdt=wdt, key0=key0: e.matmul(
                        pt[:, off:off + wdt], b.ones_t[:], mrow[:, key0:key0 + wdt],
                        start=False, stop=True),
                        inc=(pe if gi == len(sgroups) - 1 else None))

            emit_sgroups(sg_a)
            tok_sc_p0 = tok_sc
            tr1 = P.run("vector", lambda e: e.reduce_max(srows[:, 0:1], p0[:, :], axis=AX.X, negate=True),
                        waits=[tok_sc_p0], inc=dve)
            emit_sgroups(sg_b)
            tr2 = P.run("vector", lambda e, hi=hi: e.reduce_max(srows[:, 1:2], p1[:, 0:hi], axis=AX.X, negate=True),
                        waits=[tok_sc, tr1], inc=dve)
            tok_rm = P.run("vector", lambda e: e.tensor_tensor(
                srows[:, 2:3], srows[:, 0:1], srows[:, 1:2], ALU.min), waits=[tr2], inc=dve)
            P.run("scalar", lambda e, slot=slot: e.activation(
                probs[slot][:, 0:1024], p0[:, :], AF.Exp, bias=srows[:, 2:3],
                accum_out=srows[:, 3:4]), waits=[tok_rm, probs_free[slot]])
            tok_exp = P.run("scalar", lambda e, slot=slot, hi=hi: e.activation(
                probs[slot][:, 1024:1024 + hi], p1[:, 0:hi], AF.Exp, bias=srows[:, 2:3],
                accum_out=srows[:, 4:5]), inc=act)
            scores_free[0] = tok_exp
            scores_free[1] = tok_exp
            ta1 = P.run("vector", lambda e: e.tensor_add(srows[:, 5:6], srows[:, 3:4], srows[:, 4:5]),
                        waits=[tok_exp], inc=dve)
            ta2 = P.run("vector", lambda e: e.reciprocal(srows[:, 6:7], srows[:, 5:6]),
                        waits=[ta1], inc=dve)
            tok_pn = P.run("vector", lambda e, slot=slot, hi=hi: e.tensor_scalar_mul(
                probs[slot][:, 0:1024 + hi], probs[slot][:, 0:1024 + hi], srows[:, 6:7]),
                waits=[ta2], inc=dve)
            if pending is not None:
                emit_tail(*pending)
            pending = (qt, h, tok_pn)
        mrow_free = tok_sc
        diag_free = tok_sc
    emit_tail(*pending)
    tok_ao_done = tok_ao

    pp_free2 = [scores_free[0], scores_free[1]]

    def ev_o(ob, pt, tok_pe):
        return P.run("vector", lambda e, ob=ob, pt=pt: e.scalar_tensor_tensor(
            x[:, ob * T:(ob + 1) * T], pt[:, :], bocb[:, ob:ob + 1], x[:, ob * T:(ob + 1) * T],
            ALU.add, ALU.add), waits=[tok_pe, tok_bo], inc=dve)
    ev = emit_proj(P, b, wst, wst_free, [p0, p1], pp_free2, ao, tok_ao_done, w["wo"], D, ev_o)
    P.dbg = {"qf": qf, "kfull": kfull, "vfull": vfull, "vtm": vtm, "ao": ao, "pscr": pscr, "atT": atT}
    P.dbg_srows = srows
    return ev[-1]



def emit_moe(P, b, es, pspool, x, xn, xn_toks, st, w, half_scale=0.5, pairs=(0, 1, 2, 3)):
    nc = P.nc
    pe, act, dve, hw, sw = (P.ctxs[k] for k in ("pe", "act", "dve", "hw", "sw"))
    xn_done = xn_toks[-1]
    p0, p1, p2, p3 = pspool
    NIB = I1 // 128
    DN2 = (6, 5)

    wg = es(nc.sbuf_tensor(uname("s"), [128, 2 * NCH * 256], BF16))
    wd = es(nc.sbuf_tensor(uname("s"), [128, 2 * 6 * 512], BF16))
    sgb = es(nc.sbuf_tensor(uname("s"), [128, 2 * T], BF16))
    actb = es(nc.sbuf_tensor(uname("s"), [128, 2 * NIB * T], BF16))
    rsb = es(nc.sbuf_tensor(uname("s"), [128, 128], F32))
    lsb = es(nc.sbuf_tensor(uname("s"), [128, 8], F32))
    msk = es(nc.sbuf_tensor(uname("s"), [128, 88], F32))
    mwbf = es(nc.sbuf_tensor(uname("s"), [128, 64], BF16))
    rcols = es(nc.sbuf_tensor(uname("s"), [128, 24], F32))
    rrow = es(nc.sbuf_tensor(uname("s"), [1, 16], F32))
    mrow_sb = es(nc.sbuf_tensor(uname("s"), [1, T], F32))

    tok_r = P.run("sync", lambda e: e.dma_start(out=rsb[:], in_=w["rcb"][:, :]), inc=hw)
    for hc in range(NCH):
        wl = [tok_r]
        if hc == 0:
            wl.append(xn_done)
        tok_rw = P.run("tensor", lambda e, hc=hc: e.matmul(
            p2[0:1, 0:8], b.ones_col_f32[:], rsb[:, hc * 8:(hc + 1) * 8],
            start=(hc == 0), stop=(hc == NCH - 1)),
            waits=wl, inc=(pe if hc == NCH - 1 else None))
    t_rw = P.run("vector", lambda e: e.tensor_copy(rrow[0:1, 0:8], p2[0:1, 0:8]),
                 waits=[tok_rw], inc=dve)
    t_nrw = P.run("vector", lambda e: e.tensor_scalar_mul(rrow[0:1, 8:16], rrow[0:1, 0:8], -1.0),
                  waits=[t_rw], inc=dve)
    for tt in range(8):
        wl = [st.tok_r]
        if tt == 0:
            wl.append(xn_done)
        P.run("tensor", lambda e, tt=tt: e.matmul(
            p3[:, tt:tt + 1], st.row_r[0:1, tt * 128:(tt + 1) * 128], b.one_one[0:1, 0:1],
            start=True, stop=True), waits=wl)
        tok_rc = P.run("tensor", lambda e, tt=tt: e.matmul(
            p3[:, 8 + tt:9 + tt], st.row_m[0:1, tt * 128:(tt + 1) * 128], b.one_one[0:1, 0:1],
            start=True, stop=True), waits=[st.tok_m], inc=(pe if tt == 7 else None))
    t_rc = P.run("vector", lambda e: e.tensor_copy(rcols[:, 0:16], p3[:, 0:16]),
                 waits=[tok_rc], inc=dve)
    tok_brw = P.run("tensor", lambda e: e.matmul(
        p3[:, 16:24], b.ones_row_f32[:], rrow[0:1, 0:8], start=True, stop=True),
        waits=[t_rw, t_rc], inc=pe)
    t_brw = P.run("vector", lambda e: e.tensor_copy(rcols[:, 16:24], p3[:, 16:24]),
                  waits=[tok_brw], inc=dve)
    t_mask_done = None
    t_lsb_free = t_rc
    p2_free = None
    for tt in range(8):
        for hc in range(NCH):
            wl = [t_rc, t_brw]
            if hc == 0:
                wl.append(p2_free)
            tok_lg = P.run("tensor", lambda e, tt=tt, hc=hc: e.matmul(
                p2[:, 0:8], x[:, hc * T + tt * 128: hc * T + tt * 128 + 128],
                rsb[:, hc * 8:(hc + 1) * 8],
                start=(hc == 0), stop=(hc == NCH - 1)), waits=wl,
                inc=(pe if hc == NCH - 1 else None))
        t_l0 = P.run("vector", lambda e: e.tensor_copy(lsb[:], p2[:, 0:8]),
                     waits=[tok_lg, t_lsb_free], inc=dve)
        p2_free = t_l0
        t_fx = P.run("vector", lambda e, tt=tt: e.tensor_scalar_mul(
            msk[:, 76:84], rcols[:, 16:24], rcols[:, 8 + tt:9 + tt]),
            waits=[t_brw, t_l0], inc=dve)
        t_l = P.run("vector", lambda e: e.tensor_sub(lsb[:], lsb[:], msk[:, 76:84]),
                    waits=[t_fx], inc=dve)
        t_m0 = P.run("vector", lambda e: e.reduce_max(msk[:, 84:85], lsb[:], axis=AX.X),
                     waits=[t_l], inc=dve)
        t_eq = P.run("vector", lambda e: e.tensor_single_scalar(
            msk[:, 72:80], lsb[:], msk[:, 84:85], ALU.is_equal), waits=[t_m0], inc=dve)
        t_t2 = P.run("vector", lambda e: e.scalar_tensor_tensor(
            msk[:, 64:72], msk[:, 72:80], -1e30, lsb[:], ALU.mult, ALU.add),
            waits=[t_eq], inc=dve)
        t_m1 = P.run("vector", lambda e: e.reduce_max(msk[:, 85:86], msk[:, 64:72], axis=AX.X),
                     waits=[t_t2], inc=dve)
        t_gap = P.run("vector", lambda e: e.tensor_sub(msk[:, 86:87], msk[:, 84:85], msk[:, 85:86]),
                      waits=[t_m1], inc=dve)
        t_w = P.run("scalar", lambda e, tt=tt: e.activation(
            msk[:, 87:88], msk[:, 86:87], AF.Sigmoid, scale=rcols[:, tt:tt + 1]),
            waits=[t_gap, t_rc], inc=act)
        t_mw = P.run("vector", lambda e, tt=tt: e.tensor_scalar_mul(
            msk[:, tt * 8:(tt + 1) * 8], msk[:, 72:80], msk[:, 87:88]),
            waits=[t_w], inc=dve)
        t_mask_done = P.run("vector", lambda e, tt=tt: e.tensor_copy(
            mwbf[:, tt * 8:(tt + 1) * 8], msk[:, tt * 8:(tt + 1) * 8]),
            waits=[t_mw], inc=dve)
        t_lsb_free = t_mask_done

    GW = 256
    NBB = 6
    bwidth = [256, 256, 256, 256, 256, 128]
    boff = [0, 256, 512, 768, 1024, 1280]

    def dma_gu(mat, e_, bb, slot, wait):
        i0 = boff[bb]
        bw = bwidth[bb]
        src = mat[e_].rearrange("(c p) i -> p c i", p=128)[:, :, i0:i0 + bw]
        return P.run("gpsimd", lambda e, src=src, slot=slot, bw=bw: e.dma_start(
            out=wg[:, slot * NCH * GW: slot * NCH * GW + NCH * bw].rearrange(
                "p (c j) -> p c j", c=NCH),
            in_=src), waits=[wait], inc=sw)

    def dma_dn(e_, hq, s2, slot, wait):
        c0 = 0 if s2 == 0 else DN2[0]
        ncs = DN2[s2]
        src = w["downT"][e_].rearrange("(c p) h -> p c h", p=128)[
            :, c0:c0 + ncs, hq * 512:(hq + 1) * 512]
        return P.run("gpsimd", lambda e, src=src, slot=slot, ncs=ncs: e.dma_start(
            out=wd[:, slot * 6 * 512: slot * 6 * 512 + ncs * 512].rearrange(
                "p (c j) -> p c j", c=ncs),
            in_=src), waits=[wait], inc=sw)

    gu_seq = []
    for e_ in range(NE):
        for bb in range(NBB):
            gu_seq.append(("g", e_, bb))
            gu_seq.append(("u", e_, bb))
    gu_toks, gu_slot = {}, {}
    slot_free = [None, None]
    for idx in range(2):
        mt, e_, bb = gu_seq[idx]
        gu_toks[(mt, e_, bb)] = dma_gu(w["gateT"] if mt == "g" else w["upT"], e_, bb,
                                       idx % 2, slot_free[idx % 2])
        gu_slot[(mt, e_, bb)] = idx % 2
    gu_next = 2

    dn_seq = [(pr_ * 2 + ei_, hq_, s2_) for pr_ in range(4) for hq_ in range(4)
              for ei_ in range(2) for s2_ in range(2)]
    dn_toks, dn_slot = {}, {}
    dn_slot_free = [None, None]
    dn_slot_free[1] = None
    t_wdinit = P.run("vector", lambda e: e.memset(wd[:], 0.0), inc=dve)
    dn_toks[dn_seq[0]] = dma_dn(0, 0, 0, 0, t_wdinit)
    dn_slot[dn_seq[0]] = 0
    dn_next = 1

    ps_free = [None, None, None, None]
    ps_free[2] = p2_free
    ps_free[3] = t_rc
    actb_free = [None, None]
    sgb_free = [None, None]
    tok_ev = None

    for pr in range(4):
        if pr not in pairs:
            pass
        acts_done = []
        bm_toks = []
        for ei in range(2):
            e_ = pr * 2 + ei

            def emit_bmask(e_=e_):
                for tt in range(8):
                    wl = [t_mask_done, ps_free[2]]
                    P.run("tensor", lambda e, e_=e_, tt=tt: e.matmul(
                        p2[0:1, tt * 128:(tt + 1) * 128], mwbf[:, tt * 8 + e_:tt * 8 + e_ + 1],
                        b.ident_bf[:], start=True, stop=True),
                        waits=wl, inc=(pe if tt == 7 else None))
                tok_mre = P.run("vector", lambda e: e.tensor_copy(mrow_sb[0:1, :], p2[0:1, :]),
                                waits=[(pe, pe.n)], inc=dve)
                for hh in range(2):
                    tok_bm = P.run("tensor", lambda e, hh=hh: e.matmul(
                        p2[:, hh * 512:hh * 512 + 512], b.ones_row_f32[:],
                        mrow_sb[0:1, hh * 512:hh * 512 + 512], start=True, stop=True),
                        waits=[tok_mre], inc=(pe if hh == 1 else None))
                return tok_bm

            tok_bm = None
            tok_act = None
            for bb in range(NBB):
                nblk = bwidth[bb] // 128
                for mt in ("g", "u"):
                    key = (mt, e_, bb)
                    tw = gu_toks.pop(key)
                    slot = gu_slot.pop(key)
                    base = slot * NCH * GW
                    bw = bwidth[bb]
                    for hc in range(NCH):
                        for ib in range(nblk):
                            for hh in range(2):
                                wl = [tw, xn_done]
                                if hc == 0 and hh == 0:
                                    wl.append(ps_free[ib])
                                tok_mm = P.run("tensor", lambda e, hc=hc, ib=ib, hh=hh, base=base, bw=bw: e.matmul(
                                    pspool[ib][:, hh * 512:hh * 512 + 512],
                                    wg[:, base + hc * bw + ib * 128: base + hc * bw + ib * 128 + 128],
                                    xn[:, hc * T + hh * 512: hc * T + hh * 512 + 512],
                                    start=(hc == 0), stop=(hc == NCH - 1)),
                                    waits=wl,
                                    inc=(pe if (hc == NCH - 1 and ib == nblk - 1 and hh == 1) else None))
                    slot_free[slot] = tok_mm
                    if gu_next < len(gu_seq):
                        mt2, e2, bb2 = gu_seq[gu_next]
                        s2 = gu_next % 2
                        gu_toks[(mt2, e2, bb2)] = dma_gu(
                            w["gateT"] if mt2 == "g" else w["upT"], e2, bb2, s2, slot_free[s2])
                        gu_slot[(mt2, e2, bb2)] = s2
                        gu_next += 1
                    if bb == 0 and mt == "g" and tok_bm is None:
                        tok_bm = emit_bmask()
                    ibase = boff[bb] // 128
                    if mt == "g":
                        for ib in range(nblk):
                            tok_sg = P.run("scalar", lambda e, ib=ib, ibase=ibase: e.activation(
                                sgb[:, ib * T:(ib + 1) * T], pspool[ib][:, :], AF.Silu),
                                waits=[tok_mm, sgb_free[ib]], inc=act)
                            ps_free[ib] = tok_sg
                    else:
                        for ib in range(nblk):
                            tok_act = P.run("vector", lambda e, ib=ib, ibase=ibase, ei=ei: e.scalar_tensor_tensor(
                                actb[:, (ei * NIB + ibase + ib) * T:(ei * NIB + ibase + ib + 1) * T],
                                pspool[ib][:, :], 1.0, sgb[:, ib * T:(ib + 1) * T],
                                ALU.mult, ALU.mult),
                                waits=[tok_mm, tok_sg, actb_free[ei]], inc=dve)
                            ps_free[ib] = tok_act
                            sgb_free[ib] = tok_act
            t_am = None
            for ic in range(NIB):
                t_am = P.run("vector", lambda e, ic=ic, ei=ei: e.tensor_mul(
                    actb[:, (ei * NIB + ic) * T:(ei * NIB + ic + 1) * T],
                    actb[:, (ei * NIB + ic) * T:(ei * NIB + ic + 1) * T],
                    p2[:, :]), waits=[tok_act, tok_bm], inc=dve)
            acts_done.append(t_am)
            ps_free[2] = t_am
        tok_acts = acts_done[-1]
        for hq in range(4):
            tok_dmm = None
            for ei in range(2):
                e_ = pr * 2 + ei
                for s2 in range(2):
                    key = (e_, hq, s2)
                    tw = dn_toks.pop(key)
                    dslot = dn_slot.pop(key)
                    dbase = dslot * 6 * 512
                    c0 = 0 if s2 == 0 else DN2[0]
                    ncs = DN2[s2]
                    for icl in range(ncs):
                        ic = c0 + icl
                        first = (ei == 0 and ic == 0)
                        for hcq in range(4):
                            for hh in range(2):
                                wl = [tw, acts_done[ei]]
                                if first and hh == 0:
                                    wl.append(ps_free[hcq])
                                last_w = (ei == 1 and s2 == 1 and icl == ncs - 1)
                                tile_last = (icl == ncs - 1 and hcq == 3 and hh == 1)
                                tok_dmm = P.run("tensor", lambda e, icl=icl, ic=ic, hcq=hcq, hh=hh, dbase=dbase, ei=ei, first=first, last_w=last_w: e.matmul(
                                    pspool[hcq][:, hh * 512:hh * 512 + 512],
                                    wd[:, dbase + icl * 512 + hcq * 128: dbase + icl * 512 + hcq * 128 + 128],
                                    actb[:, (ei * NIB + ic) * T + hh * 512: (ei * NIB + ic) * T + hh * 512 + 512],
                                    start=first, stop=last_w),
                                    waits=wl, inc=(pe if tile_last else None))
                    dn_slot_free[dslot] = tok_dmm
                    if dn_next < len(dn_seq):
                        e2, hq2, s22 = dn_seq[dn_next]
                        sl2 = dn_next % 2
                        dn_toks[(e2, hq2, s22)] = dma_dn(e2, hq2, s22, sl2, dn_slot_free[sl2])
                        dn_slot[(e2, hq2, s22)] = sl2
                        dn_next += 1
            for hcq in range(4):
                hc = hq * 4 + hcq
                tok_ev = P.run("vector", lambda e, hcq=hcq, hc=hc: e.scalar_tensor_tensor(
                    x[:, hc * T:(hc + 1) * T], pspool[hcq][:, :],
                    half_scale, x[:, hc * T:(hc + 1) * T],
                    ALU.mult, ALU.add),
                    waits=[tok_dmm], inc=dve)
                ps_free[hcq] = tok_ev
        actb_free = [tok_dmm, tok_dmm]
    P.dbg_mw = mwbf
    P.dbg_actb = actb
    P.dbg_wd = wd
    return tok_ev


def moe_weight_arrays(inputs):
    rw = np.asarray(inputs["L1_ffn_router_w"], np.float32)
    rcb = np.empty((128, 128), np.float32)
    for hcc in range(NCH):
        rcb[:, hcc * 8:(hcc + 1) * 8] = rw[:, hcc * 128:(hcc + 1) * 128].T
    gateT = np.ascontiguousarray(np.asarray(inputs["L1_ffn_gate_w"], np.float32).transpose(0, 2, 1))
    upT = np.ascontiguousarray(np.asarray(inputs["L1_ffn_up_w"], np.float32).transpose(0, 2, 1))
    downT = np.ascontiguousarray(np.asarray(inputs["L1_ffn_down_w"], np.float32).transpose(0, 2, 1))
    return {"rcb": np.ascontiguousarray(rcb), "gateT": gateT, "upT": upT, "downT": downT}


def declare_moe_weights(nc):
    w = {}
    w["rcb"] = nc.dram_tensor("w_rcb", [128, 128], F32, kind="ExternalInput").ap()
    w["gateT"] = nc.dram_tensor("w_m_gateT", [NE, D, I1], F32, kind="ExternalInput").ap()
    w["upT"] = nc.dram_tensor("w_m_upT", [NE, D, I1], F32, kind="ExternalInput").ap()
    w["downT"] = nc.dram_tensor("w_m_downT", [NE, I1, D], F32, kind="ExternalInput").ap()
    return w


def build_moe_launch(debug=False):
    from contextlib import ExitStack
    nc = bass.Bass("TRN2", target_bir_lowering=False, detect_race_conditions=False)
    xin = nc.dram_tensor("xin", [128, NCH * T], F32, kind="ExternalInput").ap()
    xout = nc.dram_tensor("xout", [128, NCH * T], F32, kind="ExternalOutput").ap()
    ident = nc.dram_tensor("w_ident", [128, 128], BF16, kind="ExternalInput").ap()
    w = declare_moe_weights(nc)
    with ExitStack() as stack:
        es = stack.enter_context
        x = es(nc.sbuf_tensor("x", [128, NCH * T], F32))
        xn = es(nc.sbuf_tensor("xn", [128, NCH * T], BF16))
        b = alloc_consts(nc, es)
        b.ident_bf = es(nc.sbuf_tensor("ident_bf", [128, 128], BF16))
        ctxs = make_ctxs(nc, es)
        P = Prog(nc, ctxs)
        hw = ctxs["hw"]
        setup_consts(P, b)
        tok_x = P.run("sync", lambda e: e.dma_start(out=x[:], in_=xin[:, :]), inc=hw)
        P.run("sync", lambda e: e.dma_start(out=b.ident_bf[:], in_=ident[:, :]), inc=hw)
        pspool = [es(nc.psum_tensor(f"pspool{i}", [128, T], F32)) for i in range(4)]
        st = alloc_ln_sbuf(nc, es)
        ln_psum_views(st, pspool[2], pspool[3])
        xn_toks, _ = emit_ln(P, b, st, x[:], xn[:], tok_x)
        tok = emit_moe(P, b, es, pspool, x[:], xn[:], xn_toks, st, w)
        alltoks = [tok, (P.ctxs["dve"], P.ctxs["dve"].n), (P.ctxs["act"], P.ctxs["act"].n),
                   (P.ctxs["pe"], P.ctxs["pe"].n)]
        tst = P.run("sync", lambda e: e.dma_start(out=xout[:, :], in_=x[:]),
                    waits=alltoks, inc=hw)
        if debug:
            dm = nc.dram_tensor("dbg_mask", [128, 64], BF16, kind="ExternalOutput").ap()
            tst = P.run("sync", lambda e, dm=dm: e.dma_start(out=dm[:, :], in_=P.dbg_mw[:]), inc=hw)
            dl = nc.dram_tensor("dbg_acts", [128, 2 * 11 * T], BF16, kind="ExternalOutput").ap()
            tst = P.run("sync", lambda e, dl=dl: e.dma_start(out=dl[:, :], in_=P.dbg_actb[:]), inc=hw)
            dwd = nc.dram_tensor("dbg_wd", [128, 2 * 6 * 512], BF16, kind="ExternalOutput").ap()
            tst = P.run("sync", lambda e, dwd=dwd: e.dma_start(out=dwd[:, :], in_=P.dbg_wd[:]), inc=hw)
        P.run("sync", lambda e: e.nop(), waits=[tst])
        P.emit()
    return nc



def emit_attn1(P, b, es, pspool, x, xn, xn_toks, st, w, qsc, cc_in, cc_out,
               groups=GROUPS):
    nc = P.nc
    pe, act, dve, hw, sw, cc = (P.ctxs[k] for k in ("pe", "act", "dve", "hw", "sw", "cc"))
    xn_done = xn_toks[-1]
    p0, p1, p2 = pspool

    bt = [es(nc.psum_tensor(uname("p"), [128, 1024], BF16)) for i in range(2)]
    wq_st = es(nc.sbuf_tensor(uname("s"), [128, 2 * 2048], BF16))
    wv_st = es(nc.sbuf_tensor(uname("s"), [128, 2 * 2048], BF16))
    uf = es(nc.sbuf_tensor(uname("s"), [128, NCH * T], BF16))
    qh = es(nc.sbuf_tensor(uname("s"), [128, 2 * T], BF16))
    vh = es(nc.sbuf_tensor(uname("s"), [128, 2 * T], BF16))
    esc = es(nc.sbuf_tensor(uname("s"), [128, 2 * T], BF16))
    tsc = es(nc.sbuf_tensor(uname("s"), [128, 2048], BF16))
    kvp = es(nc.sbuf_tensor(uname("s"), [128, 2048], BF16))
    kvq = es(nc.sbuf_tensor(uname("s"), [128, 2048], BF16))
    bias = es(nc.sbuf_tensor(uname("s"), [128, 6 * NCH], F32))
    ao = xn

    tok_b = P.run("sync", lambda e: e.dma_start(out=bias[:], in_=w["bias6"][:, :]), inc=hw)

    def dma_wt(wap, dst, ob, slot, wait):
        src = wap.rearrange("(c p) o -> p c o", p=128)[:, :, ob * 128:(ob + 1) * 128]
        return P.run("gpsimd", lambda e, src=src, slot=slot, dst=dst: e.dma_start(
            out=dst[:, slot * 2048:(slot + 1) * 2048].rearrange("p (c j) -> p c j", c=NCH),
            in_=src), waits=[wait], inc=sw)

    wq_free = [None, None]
    wv_free = [None, None]
    wq_toks = {0: dma_wt(w["wqk"], wq_st, 0, 0, None), 1: dma_wt(w["wqk"], wq_st, 1, 1, None)}
    wv_toks = {0: dma_wt(w["wv"], wv_st, 0, 0, None), 1: dma_wt(w["wv"], wv_st, 1, 1, None)}
    pq_free = None
    pv_free = None
    qh_free = [None, None]
    vh_free = [None, None]
    bt_free = [None, None]
    tsc_free = None
    kv_group_free = [None, None]
    kv_ev = [None, None]
    for h in range(NCH):
        slot = h % 2
        twq = wq_toks.pop(h)
        for hc in range(NCH):
            for hh in range(2):
                wl = [twq, xn_done, tok_b]
                if hc == 0 and hh == 0:
                    wl.append(pq_free)
                tok_qmm = P.run("tensor", lambda e, hc=hc, hh=hh, slot=slot: e.matmul(
                    p0[:, hh * 512:hh * 512 + 512],
                    wq_st[:, slot * 2048 + hc * 128: slot * 2048 + hc * 128 + 128],
                    xn[:, hc * T + hh * 512: hc * T + hh * 512 + 512],
                    start=(hc == 0), stop=(hc == NCH - 1)),
                    waits=wl, inc=(pe if (hc == NCH - 1 and hh == 1) else None))
        wq_free[slot] = tok_qmm
        if h + 2 < NCH:
            wq_toks[h + 2] = dma_wt(w["wqk"], wq_st, h + 2, slot, wq_free[slot])
        t_e1 = P.run("vector", lambda e, slot=slot, h=h: e.tensor_scalar(
            esc[:, slot * T:(slot + 1) * T], p0[:, :], bias[:, h:h + 1], 0.0,
            ALU.add, ALU.min), waits=[tok_qmm, qh_free[slot]], inc=dve)
        t_e3 = P.run("vector", lambda e, slot=slot, h=h: e.tensor_scalar(
            qh[:, slot * T:(slot + 1) * T], p0[:, :], bias[:, h:h + 1], 0.0,
            ALU.add, ALU.max), waits=[t_e1], inc=dve)
        pq_free = t_e3
        t_e2 = P.run("scalar", lambda e, slot=slot: e.activation(
            esc[:, slot * T:(slot + 1) * T], esc[:, slot * T:(slot + 1) * T], AF.Exp),
            waits=[t_e1], inc=act)
        t_q = P.run("vector", lambda e, slot=slot: e.scalar_tensor_tensor(
            qh[:, slot * T:(slot + 1) * T], esc[:, slot * T:(slot + 1) * T], -1.0,
            qh[:, slot * T:(slot + 1) * T], ALU.add, ALU.add),
            waits=[t_e2, t_e3], inc=dve)
        t_qs = P.run("sync", lambda e, h=h, slot=slot: e.dma_start(
            out=qsc[:, h * T:(h + 1) * T], in_=qh[:, slot * T:(slot + 1) * T]),
            waits=[t_q], inc=hw)
        twv = wv_toks.pop(h)
        for hc in range(NCH):
            for hh in range(2):
                wl = [twv, xn_done, tok_b]
                if hc == 0 and hh == 0:
                    wl.append(pv_free)
                tok_vmm = P.run("tensor", lambda e, hc=hc, hh=hh, slot=slot: e.matmul(
                    p1[:, hh * 512:hh * 512 + 512],
                    wv_st[:, slot * 2048 + hc * 128: slot * 2048 + hc * 128 + 128],
                    xn[:, hc * T + hh * 512: hc * T + hh * 512 + 512],
                    start=(hc == 0), stop=(hc == NCH - 1)),
                    waits=wl, inc=(pe if (hc == NCH - 1 and hh == 1) else None))
        wv_free[slot] = tok_vmm
        if h + 2 < NCH:
            wv_toks[h + 2] = dma_wt(w["wv"], wv_st, h + 2, slot, wv_free[slot])
        t_v = P.run("scalar", lambda e, slot=slot, h=h: e.activation(
            vh[:, slot * T:(slot + 1) * T], p1[:, :], AF.Silu,
            bias=bias[:, 1 * NCH + h:1 * NCH + h + 1]),
            waits=[tok_vmm, vh_free[slot]], inc=act)
        pv_free = t_v
        kvreg = p2[:, (h % 8) * 128:(h % 8) * 128 + 128]
        bank = (h % 8) // 4
        for tc in range(8):
            P.run("tensor", lambda e, tc=tc, slot=slot: e.transpose(
                bt[0][:, tc * 128:(tc + 1) * 128],
                qh[:, slot * T + tc * 128: slot * T + tc * 128 + 128],
                b.ident_bf[:]), waits=[t_q, bt_free[0]] if tc == 0 else [])
        tok_trq = P.run("tensor", lambda e: e.engine_nop() if False else e.nop())
        for tc in range(8):
            tok_tr = P.run("tensor", lambda e, tc=tc, slot=slot: e.transpose(
                bt[1][:, tc * 128:(tc + 1) * 128],
                vh[:, slot * T + tc * 128: slot * T + tc * 128 + 128],
                b.ident_bf[:]), waits=[t_v, bt_free[1]] if tc == 0 else [],
                inc=(pe if tc == 7 else None))
        t_ts = P.run("vector", lambda e: e.tensor_copy(tsc[:, 0:1024], bt[0][:, :]),
                     waits=[tok_tr, tsc_free], inc=dve)
        t_ts2 = P.run("vector", lambda e: e.tensor_copy(tsc[:, 1024:2048], bt[1][:, :]),
                      waits=[t_ts], inc=dve)
        bt_free[0] = t_ts
        bt_free[1] = t_ts2
        for tc in range(8):
            wl = [t_ts2]
            if tc == 0:
                wl.append(kv_group_free[bank])
            tok_kv = P.run("tensor", lambda e, tc=tc, kvreg=kvreg: e.matmul(
                kvreg, tsc[:, tc * 128:(tc + 1) * 128], tsc[:, 1024 + tc * 128:1024 + tc * 128 + 128],
                start=(tc == 0), stop=(tc == 7)),
                waits=wl, inc=(pe if tc == 7 else None))
        tsc_free = tok_kv
        qh_free[slot] = tok_kv
        vh_free[slot] = tok_kv
        if h % 4 == 3:
            t_kve = P.run("vector", lambda e, h=h: e.tensor_copy(
                kvp[:, (h - 3) * 128:(h + 1) * 128],
                p2[:, ((h % 8) - 3) * 128:((h % 8) + 1) * 128]),
                waits=[tok_kv], inc=dve)
            kv_group_free[bank] = t_kve
            kv_ev[bank] = t_kve
    tok_kvp = kv_ev[1]

    t1 = P.run("sync", lambda e: e.dma_start(out=cc_in[:, :], in_=kvp[:]),
               waits=[kv_ev[0], kv_ev[1]], inc=hw)
    tcc = P.run("gpsimd", lambda e, groups=groups: e.collective_compute(
        "AllGather", ALU.bypass, replica_groups=groups,
        ins=[cc_in.opt()], outs=[cc_out.opt()]), waits=[t1], inc=cc, inc_k=1)
    t2a = P.run("sync", lambda e: e.dma_start(out=kvq[:, :], in_=cc_out[0:128, :]),
                waits=[tcc], inc=hw)
    t2b = P.run("sync", lambda e: e.dma_start(out=kvp[:, :], in_=cc_out[128:256, :]),
                inc=hw)
    tok_kx = (hw, hw.n)
    t_kv = P.run("vector", lambda e: e.tensor_add(kvp[:], kvp[:], kvq[:]),
                 waits=[tok_kx], inc=dve)

    wst_free = [wq_free[0], wq_free[1]]
    pp_free = [pq_free, pv_free]

    def ev_u(ob, pt, tok_pe):
        return P.run("scalar", lambda e, ob=ob, pt=pt: e.activation(
            uf[:, ob * T:(ob + 1) * T], pt[:, :], AF.Silu,
            bias=bias[:, 2 * NCH + ob:2 * NCH + ob + 1]),
            waits=[tok_pe, tok_b], inc=act)
    evu = emit_proj(P, b, wq_st, wst_free, [p0, p1], pp_free, xn, xn_done, w["wu"], D, ev_u)
    tok_u = evu[-1]

    qb_toks = {}
    for h in range(2):
        qb_toks[h] = P.run("sync", lambda e, h=h: e.dma_start(
            out=qh[:, (h % 2) * T:(h % 2) * T + T], in_=qsc[:, h * T:(h + 1) * T]),
            waits=[qh_free[h % 2], tok_u], inc=hw)
    ao_toks = []
    for h in range(NCH):
        slot = h % 2
        tq = qb_toks.pop(h)
        for hh in range(2):
            wl = [tq, t_kv]
            if hh == 0:
                wl.append(pp_free[slot])
            tok_o = P.run("tensor", lambda e, h=h, hh=hh, slot=slot: e.matmul(
                (p0 if slot == 0 else p1)[:, hh * 512:hh * 512 + 512],
                kvp[:, h * 128:(h + 1) * 128],
                qh[:, slot * T + hh * 512: slot * T + hh * 512 + 512],
                start=True, stop=True),
                waits=wl, inc=(pe if hh == 1 else None))
        t_ao = P.run("scalar", lambda e, h=h, slot=slot: e.activation(
            ao[:, h * T:(h + 1) * T], (p0 if slot == 0 else p1)[:, :], AF.Copy),
            waits=[tok_o], inc=act)
        pp_free[slot] = t_ao
        qh_free[slot] = t_ao
        if h + 2 < NCH:
            qb_toks[h + 2] = P.run("sync", lambda e, h=h, slot=slot: e.dma_start(
                out=qh[:, slot * T:(slot + 1) * T], in_=qsc[:, (h + 2) * T:(h + 3) * T]),
                waits=[qh_free[slot]], inc=hw)
        ao_toks.append(t_ao)
    tok_ao = ao_toks[-1]

    st.cbuf_free = [None, None]
    st.qbuf_free = [None, None]
    for c in range(NCH):
        t2q = P.run("scalar", lambda e, c=c: e.activation(
            st.qbuf[:, (c % 2) * T:(c % 2) * T + T], ao[:, c * T:(c + 1) * T], AF.Square),
            waits=[ao_toks[c], st.qbuf_free[c % 2]], inc=act)
        for hh in range(2):
            wl_s1 = [ao_toks[c]]
            if c == 0 and hh == 0:
                wl_s1 += [tok_ao, t_kv]
            P.run("tensor", lambda e, c=c, hh=hh: e.matmul(
                st.ps_s1[:, hh * 512:hh * 512 + 512], b.ones_col_bf[:],
                ao[:, c * T + hh * 512: c * T + hh * 512 + 512],
                start=(c == 0), stop=(c == NCH - 1)), waits=wl_s1)
            tok_mm = P.run("tensor", lambda e, c=c, hh=hh: e.matmul(
                st.ps_s2[:, hh * 512:hh * 512 + 512], b.ones_col_bf[:],
                st.qbuf[:, (c % 2) * T + hh * 512:(c % 2) * T + hh * 512 + 512],
                start=(c == 0), stop=(c == NCH - 1)),
                waits=[t2q], inc=pe)
        st.qbuf_free[c % 2] = tok_mm
    tm = P.run("vector", lambda e: e.tensor_scalar_mul(st.row_m[:], st.ps_s1[:, :], 1.0 / D),
               waits=[tok_mm], inc=dve)
    te = P.run("vector", lambda e: e.tensor_scalar_mul(st.row_a[:], st.ps_s2[:, :], 1.0 / D),
               waits=[tm], inc=dve)
    tq_ = P.run("scalar", lambda e: e.activation(st.row_b[:], st.row_m[:], AF.Square),
                waits=[tm], inc=act)
    tv_ = P.run("vector", lambda e: e.tensor_sub(st.row_a[:], st.row_a[:], st.row_b[:]),
                waits=[te, tq_], inc=dve)
    ts_ = P.run("scalar", lambda e: e.activation(st.row_b[:], st.row_a[:], AF.Sqrt, bias=b.eps[0:1, 0:1]),
                waits=[tv_], inc=act)
    tr_ = P.run("vector", lambda e: e.reciprocal(st.row_r[:], st.row_b[:]),
                waits=[ts_], inc=dve)
    tmr = P.run("vector", lambda e: e.tensor_mul(st.row_mr[:], st.row_m[:], st.row_r[:]),
                waits=[tr_], inc=dve)
    for hh in range(2):
        osl = slice(hh * 512, hh * 512 + 512)
        P.run("tensor", lambda e, osl=osl: e.matmul(
            st.ps_br[:][:, osl], b.ones_row_f32[:], st.row_r[0:1, osl],
            start=True, stop=True), waits=[tr_])
        tb = P.run("tensor", lambda e, osl=osl: e.matmul(
            st.ps_bmr[:][:, osl], b.ones_row_f32[:], st.row_mr[0:1, osl],
            start=True, stop=True), waits=[tmr], inc=pe)
    t_m_last = None
    for c in range(NCH):
        P.run("vector", lambda e, c=c: e.tensor_mul(
            st.cbuf[:, (c % 2) * T:(c % 2) * T + T], ao[:, c * T:(c + 1) * T], st.ps_br[:][:, :]),
            waits=[tb, st.cbuf_free[c % 2]])
        t_c2 = P.run("vector", lambda e, c=c: e.tensor_sub(
            st.cbuf[:, (c % 2) * T:(c % 2) * T + T], st.cbuf[:, (c % 2) * T:(c % 2) * T + T],
            st.ps_bmr[:][:, :]), inc=dve)
        t_c3 = P.run("vector", lambda e, c=c: e.tensor_scalar(
            st.qbuf[:, (c % 2) * T:(c % 2) * T + T], st.cbuf[:, (c % 2) * T:(c % 2) * T + T],
            bias[:, 4 * NCH + c:4 * NCH + c + 1], bias[:, 5 * NCH + c:5 * NCH + c + 1],
            ALU.mult, ALU.add),
            waits=[t_c2, st.qbuf_free[c % 2], tok_b], inc=dve)
        t_m_last = P.run("vector", lambda e, c=c: e.tensor_mul(
            uf[:, c * T:(c + 1) * T], uf[:, c * T:(c + 1) * T],
            st.qbuf[:, (c % 2) * T:(c % 2) * T + T]),
            waits=[t_c3, tok_u], inc=dve)
        st.cbuf_free[c % 2] = t_m_last
        st.qbuf_free[c % 2] = t_m_last
    tok_m = t_m_last

    def ev_o(ob, pt, tok_pe):
        return P.run("vector", lambda e, ob=ob, pt=pt: e.scalar_tensor_tensor(
            x[:, ob * T:(ob + 1) * T], pt[:, :], bias[:, 3 * NCH + ob:3 * NCH + ob + 1],
            x[:, ob * T:(ob + 1) * T], ALU.add, ALU.add),
            waits=[tok_pe, tok_b], inc=dve)
    ev = emit_proj(P, b, wq_st, wst_free, [p0, p1], pp_free, uf, tok_m, w["wo"], D, ev_o)
    return ev[-1]


def attn1_weight_arrays(inputs):
    wqk = np.ascontiguousarray(inputs["L1_attn_Wqk"], dtype=np.float32)
    wv = np.ascontiguousarray(inputs["L1_attn_Wv"], dtype=np.float32)
    wu = np.ascontiguousarray(inputs["L1_attn_Wu"], dtype=np.float32)
    wo = np.ascontiguousarray(inputs["L1_attn_Wo"], dtype=np.float32)
    b6 = np.zeros((128, 6 * NCH), np.float32)
    for i, nm in enumerate(["L1_attn_bqk", "L1_attn_bv", "L1_attn_bu", "L1_attn_bo",
                            "L1_attn_ln_g", "L1_attn_ln_b"]):
        b6[:, i * NCH:(i + 1) * NCH] = np.asarray(inputs[nm], np.float32).reshape(NCH, 128).T
    return {"wqk": wqk, "wv": wv, "wu": wu, "wo": wo, "bias6": np.ascontiguousarray(b6)}


def declare_attn1_weights(nc):
    w = {}
    w["wqk"] = nc.dram_tensor("w_wqk", [D, D], F32, kind="ExternalInput").ap()
    w["wv"] = nc.dram_tensor("w_wv1", [D, D], F32, kind="ExternalInput").ap()
    w["wu"] = nc.dram_tensor("w_wu1", [D, D], F32, kind="ExternalInput").ap()
    w["wo"] = nc.dram_tensor("w_wo1", [D, D], F32, kind="ExternalInput").ap()
    w["bias6"] = nc.dram_tensor("w_bias6", [128, 6 * NCH], F32, kind="ExternalInput").ap()
    return w


def build_attn1_launch(groups=GROUPS):
    from contextlib import ExitStack
    nc = bass.Bass("TRN2", target_bir_lowering=False, detect_race_conditions=False)
    xin = nc.dram_tensor("xin", [128, NCH * T], F32, kind="ExternalInput").ap()
    xout = nc.dram_tensor("xout", [128, NCH * T], F32, kind="ExternalOutput").ap()
    ident = nc.dram_tensor("w_ident", [128, 128], BF16, kind="ExternalInput").ap()
    qsc = nc.dram_tensor("q_scratch", [128, NCH * T], BF16, kind="Internal").ap()
    cc_in = nc.dram_tensor("cc1_in", [128, 2048], BF16, kind="Internal").ap()
    cc_out = nc.dram_tensor("cc1_out", [256, 2048], BF16, kind="Internal").ap()
    w = declare_attn1_weights(nc)
    with ExitStack() as stack:
        es = stack.enter_context
        x = es(nc.sbuf_tensor("x", [128, NCH * T], F32))
        xn = es(nc.sbuf_tensor("xn", [128, NCH * T], BF16))
        b = alloc_consts(nc, es)
        b.ident_bf = es(nc.sbuf_tensor("ident_bf", [128, 128], BF16))
        ctxs = make_ctxs(nc, es)
        P = Prog(nc, ctxs)
        hw = ctxs["hw"]
        setup_consts(P, b)
        tok_x = P.run("sync", lambda e: e.dma_start(out=x[:], in_=xin[:, :]), inc=hw)
        P.run("sync", lambda e: e.dma_start(out=b.ident_bf[:], in_=ident[:, :]), inc=hw)
        pspool = [es(nc.psum_tensor(f"pspool{i}", [128, T], F32)) for i in range(3)]
        st = alloc_ln_sbuf(nc, es)
        ln_psum_views(st, pspool[1], pspool[2])
        xn_toks, _ = emit_ln(P, b, st, x[:], xn[:], tok_x)
        tok = emit_attn1(P, b, es, pspool, x[:], xn[:], xn_toks, st, w, qsc, cc_in, cc_out,
                         groups=groups)
        alltoks = [tok, (P.ctxs["dve"], P.ctxs["dve"].n), (P.ctxs["act"], P.ctxs["act"].n),
                   (P.ctxs["pe"], P.ctxs["pe"].n)]
        tst = P.run("sync", lambda e: e.dma_start(out=xout[:, :], in_=x[:]),
                    waits=alltoks, inc=hw)
        P.run("sync", lambda e: e.nop(), waits=[tst])
        P.emit()
    return nc



def ffn0_weight_arrays(inputs):
    gateT = np.ascontiguousarray(inputs["L0_ffn_gate_w"].T)
    upT = np.ascontiguousarray(inputs["L0_ffn_up_w"].T)
    downT = np.ascontiguousarray(inputs["L0_ffn_down_w"].T)
    sg = np.ascontiguousarray(inputs["L0_ffn_gate_scale"].reshape(64, 128).T)
    su = np.ascontiguousarray(inputs["L0_ffn_up_scale"].reshape(64, 128).T)
    sd = np.ascontiguousarray(inputs["L0_ffn_down_scale"].reshape(16, 128).T)
    return {"gateT": gateT, "upT": upT, "downT": downT, "sg": sg, "su": su, "sd": sd}


def declare_ffn0_weights(nc):
    w = {}
    w["gateT"] = nc.dram_tensor("w_gateT", [D, I0], I32, kind="ExternalInput").ap()
    w["upT"] = nc.dram_tensor("w_upT", [D, I0], I32, kind="ExternalInput").ap()
    w["downT"] = nc.dram_tensor("w_downT", [I0, D], I32, kind="ExternalInput").ap()
    w["sg"] = nc.dram_tensor("w_sg", [128, 64], F32, kind="ExternalInput").ap()
    w["su"] = nc.dram_tensor("w_su", [128, 64], F32, kind="ExternalInput").ap()
    w["sd"] = nc.dram_tensor("w_sd", [128, 16], F32, kind="ExternalInput").ap()
    return w


def alloc_consts(nc, es):
    b = Bufs()
    b.ones_col_bf = es(nc.sbuf_tensor("ones_col_bf", [128, 1], BF16))
    b.ones_col_f32 = es(nc.sbuf_tensor("ones_col_f32", [128, 1], F32))
    b.ones_row_bf = es(nc.sbuf_tensor("ones_row_bf", [1, 128], BF16))
    b.ones_row_f32 = es(nc.sbuf_tensor("ones_row_f32", [1, 128], F32))
    b.neg128_row = es(nc.sbuf_tensor("neg128_row", [1, 128], F32))
    b.c128_row = es(nc.sbuf_tensor("c128_row", [1, 128], F32))
    b.neg128_t = es(nc.sbuf_tensor("neg128_t", [128, 128], BF16))
    b.ones_t = es(nc.sbuf_tensor("ones_t", [128, 128], BF16))
    b.one_one = es(nc.sbuf_tensor("one_one", [1, 1], F32))
    b.eps = es(nc.sbuf_tensor("c_eps", [1, 1], F32))
    return b


def make_ctxs(nc, es):
    ctxs = {}
    for name, step, dma in (("pe", 1, False), ("act", 1, False), ("dve", 1, False),
                            ("pool", 1, False), ("hw", 16, True), ("sw", 16, True),
                            ("cc", 1, True)):
        sem = es(nc.semaphore("s_" + name))
        ctxs[name] = EngCtx(name, sem, step, dma)
    return ctxs


def build_ffn0_launch():
    from contextlib import ExitStack
    nc = bass.Bass("TRN2", target_bir_lowering=False, detect_race_conditions=False)
    xin = nc.dram_tensor("xin", [128, NCH * T], F32, kind="ExternalInput").ap()
    xout = nc.dram_tensor("xout", [128, NCH * T], F32, kind="ExternalOutput").ap()
    w = declare_ffn0_weights(nc)
    with ExitStack() as stack:
        es = stack.enter_context
        x = es(nc.sbuf_tensor("x", [128, NCH * T], F32))
        xn = es(nc.sbuf_tensor("xn", [128, NCH * T], BF16))
        b = alloc_consts(nc, es)
        ctxs = make_ctxs(nc, es)
        P = Prog(nc, ctxs)
        hw = ctxs["hw"]
        setup_consts(P, b)
        tok_x = P.run("sync", lambda e: e.dma_start(out=x[:], in_=xin[:, :]), inc=hw)
        pspool = [es(nc.psum_tensor(f"pspool{i}", [128, T], F32)) for i in range(4)]
        st = alloc_ln_sbuf(nc, es)
        ln_psum_views(st, pspool[2], pspool[3])
        xn_toks, _ = emit_ln(P, b, st, x[:], xn[:], tok_x)
        tok = emit_ffn0(P, b, es, pspool, x[:], xn[:], xn_toks, w)
        tst = P.run("sync", lambda e: e.dma_start(out=xout[:, :], in_=x[:]),
                    waits=[tok, (P.ctxs["dve"], P.ctxs["dve"].n)], inc=hw)
        P.run("sync", lambda e: e.nop(), waits=[tst])
        P.emit()
    return nc


def attn0_weight_arrays(inputs):
    wq = np.ascontiguousarray(inputs["L0_attn_Wq"], dtype=np.float32)
    wk = np.ascontiguousarray(inputs["L0_attn_Wk"], dtype=np.float32)
    wv = np.ascontiguousarray(inputs["L0_attn_Wv"], dtype=np.float32)
    wo = np.ascontiguousarray(inputs["L0_attn_Wo"], dtype=np.float32)
    bo_cb = np.ascontiguousarray(np.asarray(inputs["L0_attn_bo"], np.float32).reshape(NCH, 128).T)
    return {"wq": wq, "wk": wk, "wv": wv, "wo": wo, "bo_cb": bo_cb}


def attn0_masks(parity):
    i = np.arange(128)[:, None]
    j = np.arange(128)[None, :]
    tri = np.where(j > i, np.float32(NEG), np.float32(0.0))
    mrow = np.zeros((8, 2048), np.float32)
    diag = np.zeros((8, 128, 256), np.float32)
    for qt in range(8):
        if parity == 0:
            for kc in range(qt + 1, qt + 9):
                mrow[qt, kc * 128:(kc + 1) * 128] = NEG
            diag[qt, :, 0:128] = tri
        else:
            diag[qt, :, 128:256] = tri
    return mrow.astype(ml_dtypes.bfloat16), diag.astype(ml_dtypes.bfloat16)


def declare_attn0_weights(nc):
    w = {}
    w["wq"] = nc.dram_tensor("w_wq", [D, D], F32, kind="ExternalInput").ap()
    w["wk"] = nc.dram_tensor("w_wk", [D, 128], F32, kind="ExternalInput").ap()
    w["wv"] = nc.dram_tensor("w_wv", [D, 128], F32, kind="ExternalInput").ap()
    w["wo"] = nc.dram_tensor("w_wo", [D, D], F32, kind="ExternalInput").ap()
    w["bo_cb"] = nc.dram_tensor("w_bo", [128, NCH], F32, kind="ExternalInput").ap()
    w["maskrow"] = nc.dram_tensor("w_mrow", [8, 2048], BF16, kind="ExternalInput").ap()
    w["diag"] = nc.dram_tensor("w_diag", [8, 128, 256], BF16, kind="ExternalInput").ap()
    return w


def build_attn0_launch(groups=GROUPS, debug=False, nqt=8, nh=16):
    from contextlib import ExitStack
    nc = bass.Bass("TRN2", target_bir_lowering=False, detect_race_conditions=False)
    xin = nc.dram_tensor("xin", [128, NCH * T], F32, kind="ExternalInput").ap()
    xout = nc.dram_tensor("xout", [128, NCH * T], F32, kind="ExternalOutput").ap()
    ident = nc.dram_tensor("w_ident", [128, 128], BF16, kind="ExternalInput").ap()
    cc_in = nc.dram_tensor("cc0_in", [256, T], BF16, kind="Internal").ap()
    cc_out = nc.dram_tensor("cc0_out", [512, T], BF16, kind="Internal").ap()
    w = declare_attn0_weights(nc)
    with ExitStack() as stack:
        es = stack.enter_context
        x = es(nc.sbuf_tensor("x", [128, NCH * T], F32))
        xn = es(nc.sbuf_tensor("xn", [128, NCH * T], BF16))
        b = alloc_consts(nc, es)
        b.ident_bf = es(nc.sbuf_tensor("ident_bf", [128, 128], BF16))
        ctxs = make_ctxs(nc, es)
        P = Prog(nc, ctxs)
        hw = ctxs["hw"]
        setup_consts(P, b)
        tok_x = P.run("sync", lambda e: e.dma_start(out=x[:], in_=xin[:, :]), inc=hw)
        tok_id = P.run("sync", lambda e: e.dma_start(out=b.ident_bf[:], in_=ident[:, :]), inc=hw)
        pspool = [es(nc.psum_tensor(f"pspool{i}", [128, T], F32)) for i in range(3)]
        st = alloc_ln_sbuf(nc, es)
        ln_psum_views(st, pspool[1], pspool[2])
        xn_toks, _ = emit_ln(P, b, st, x[:], xn[:], tok_x)
        P.watermark
        tok = emit_attn0(P, b, es, pspool, x[:], xn[:], xn_toks, w, cc_in, cc_out, groups=groups, nqt=nqt, nh=nh)
        alltoks = [tok, (P.ctxs["dve"], P.ctxs["dve"].n), (P.ctxs["act"], P.ctxs["act"].n),
                   (P.ctxs["pe"], P.ctxs["pe"].n)]
        tst = P.run("sync", lambda e: e.dma_start(out=xout[:, :], in_=x[:]),
                    waits=alltoks, inc=hw)
        if debug:
            for nm, tens in P.dbg.items():
                shape = [128, tens.shape[1]]
                dt = nc.dram_tensor("dbg_" + nm, shape, BF16, kind="ExternalOutput").ap()
                tst = P.run("sync", lambda e, dt=dt, tens=tens: e.dma_start(out=dt[:, :], in_=tens[:]),
                            inc=hw)
            dsr = nc.dram_tensor("dbg_srows", [128, 8], F32, kind="ExternalOutput").ap()
            tst = P.run("sync", lambda e, dsr=dsr: e.dma_start(out=dsr[:, :], in_=P.dbg_srows[:]),
                        inc=hw)
        P.run("sync", lambda e: e.nop(), waits=[tst, tok_id])
        P.emit()
    return nc



def build_fused(groups=GROUPS):
    from contextlib import ExitStack
    nc = bass.Bass("TRN2", target_bir_lowering=False, detect_race_conditions=False)
    xin = nc.dram_tensor("xin", [128, NCH * T], F32, kind="ExternalInput").ap()
    xout = nc.dram_tensor("xout", [128, NCH * T], F32, kind="ExternalOutput").ap()
    ident = nc.dram_tensor("w_ident", [128, 128], BF16, kind="ExternalInput").ap()
    qsc = nc.dram_tensor("q_scratch", [128, NCH * T], BF16, kind="Internal").ap()
    cc0_in = nc.dram_tensor("cc0_in", [256, T], BF16, kind="Internal").ap()
    cc0_out = nc.dram_tensor("cc0_out", [512, T], BF16, kind="Internal").ap()
    cc1_in = nc.dram_tensor("cc1_in", [128, 2048], BF16, kind="Internal").ap()
    cc1_out = nc.dram_tensor("cc1_out", [256, 2048], BF16, kind="Internal").ap()
    w_f0 = declare_ffn0_weights(nc)
    w_a0 = declare_attn0_weights(nc)
    w_m = declare_moe_weights(nc)
    w_a1 = declare_attn1_weights(nc)
    with ExitStack() as stack:
        es = stack.enter_context
        x = es(nc.sbuf_tensor("x", [128, NCH * T], F32))
        xn = es(nc.sbuf_tensor("xn", [128, NCH * T], BF16))
        b = alloc_consts(nc, es)
        b.ident_bf = es(nc.sbuf_tensor("ident_bf", [128, 128], BF16))
        ctxs = make_ctxs(nc, es)
        P = Prog(nc, ctxs)
        hw = ctxs["hw"]
        st = alloc_ln_sbuf(nc, es)
        setup_consts(P, b)
        tok_x = P.run("sync", lambda e: e.dma_start(out=x[:], in_=xin[:, :]), inc=hw)
        P.run("sync", lambda e: e.dma_start(out=b.ident_bf[:], in_=ident[:, :]), inc=hw)

        def phase(kind):
            nonlocal tok_x
            with ExitStack() as pstk:
                pes = pstk.enter_context
                if kind in ("f0", "m"):
                    pspool = [pes(nc.psum_tensor(uname("p"), [128, T], F32)) for i in range(4)]
                    ln_psum_views(st, pspool[2], pspool[3])
                else:
                    pspool = [pes(nc.psum_tensor(uname("p"), [128, T], F32)) for i in range(3)]
                    ln_psum_views(st, pspool[1], pspool[2])
                xn_toks, _ = emit_ln(P, b, st, x[:], xn[:], tok_x)
                if kind == "f0":
                    tok = emit_ffn0(P, b, pes, pspool, x[:], xn[:], xn_toks, w_f0)
                elif kind == "a0":
                    tok = emit_attn0(P, b, pes, pspool, x[:], xn[:], xn_toks, w_a0,
                                     cc0_in, cc0_out, groups=groups)
                elif kind == "m":
                    tok = emit_moe(P, b, pes, pspool, x[:], xn[:], xn_toks, st, w_m)
                elif kind == "a1":
                    tok = emit_attn1(P, b, pes, pspool, x[:], xn[:], xn_toks, st, w_a1,
                                     qsc, cc1_in, cc1_out, groups=groups)
                P.emit()
            tok_x = tok
            return tok

        for kind in ("f0", "a0", "f0", "m", "a1", "m"):
            phase(kind)

        alltoks = [tok_x, (P.ctxs["dve"], P.ctxs["dve"].n), (P.ctxs["act"], P.ctxs["act"].n),
                   (P.ctxs["pe"], P.ctxs["pe"].n)]
        tst = P.run("sync", lambda e: e.dma_start(out=xout[:, :], in_=x[:]),
                    waits=alltoks, inc=hw)
        P.run("sync", lambda e: e.nop(), waits=[tst])
        P.emit()
    return nc


def fused_in_map(inputs, shards, c):
    import ml_dtypes
    ident = np.eye(128, dtype=ml_dtypes.bfloat16)
    w_f0 = inputs["_w_f0"]
    w_a0 = inputs["_w_a0"]
    w_m = inputs["_w_m"]
    w_a1 = inputs["_w_a1"]
    mrow, diag = inputs["_masks"][c % 2]
    return {"xin": shards[c], "w_ident": ident,
            "w_gateT": w_f0["gateT"], "w_upT": w_f0["upT"], "w_downT": w_f0["downT"],
            "w_sg": w_f0["sg"], "w_su": w_f0["su"], "w_sd": w_f0["sd"],
            "w_wq": w_a0["wq"], "w_wk": w_a0["wk"], "w_wv": w_a0["wv"],
            "w_wo": w_a0["wo"], "w_bo": w_a0["bo_cb"], "w_mrow": mrow, "w_diag": diag,
            "w_rcb": w_m["rcb"], "w_m_gateT": w_m["gateT"], "w_m_upT": w_m["upT"],
            "w_m_downT": w_m["downT"],
            "w_wqk": w_a1["wqk"], "w_wv1": w_a1["wv"], "w_wu1": w_a1["wu"],
            "w_wo1": w_a1["wo"], "w_bias6": w_a1["bias6"]}



def shard_x(x_full):
    shards = []
    for c in range(NCORE):
        bb, hh = c // 2, c % 2
        xs = x_full[bb, hh * T:(hh + 1) * T, :]
        xs = np.ascontiguousarray(xs.T)
        xs = xs.reshape(NCH, 128, T).transpose(1, 0, 2).reshape(128, NCH * T)
        shards.append(np.ascontiguousarray(xs))
    return shards


def unshard_x(shards):
    out = np.empty((4, 2048, D), np.float32)
    for c in range(NCORE):
        bb, hh = c // 2, c % 2
        xs = shards[c].reshape(128, NCH, T).transpose(1, 0, 2).reshape(D, T)
        out[bb, hh * T:(hh + 1) * T, :] = xs.T
    return out


def kernel(**inputs):
    x = np.asarray(inputs["L0_attn_x"], np.float32)
    shards = shard_x(x)
    prep = {"_w_f0": ffn0_weight_arrays(inputs), "_w_a0": attn0_weight_arrays(inputs),
            "_w_m": moe_weight_arrays(inputs), "_w_a1": attn1_weight_arrays(inputs),
            "_masks": [attn0_masks(0), attn0_masks(1)]}
    nc = build_fused()
    in_maps = [fused_in_map(prep, shards, c) for c in range(NCORE)]
    res = run_bass_kernel_spmd(nc, in_maps, core_ids=list(range(NCORE)))
    return unshard_x([np.asarray(res.results[c]["xout"]) for c in range(NCORE)])
